# version 09
# baseline (speedup 1.0000x reference)
import numpy as np
import concourse.bass as bass
import concourse.mybir as mybir
from concourse.bass_utils import run_bass_kernel_spmd

F32 = mybir.dt.float32
BF16 = mybir.dt.bfloat16
I32 = mybir.dt.int32
AF = mybir.ActivationFunctionType
ALU = mybir.AluOpType
AX = mybir.AxisListType

DM = 4096
NH = 128
HD = 64
NG = 8
CHK = 128
INTER = NH * HD
CONVD = INTER + 2 * NG * 128
PROJ = INTER + CONVD + NH
KW = 4
EPS = 1e-5

TOK = 2048
HALO = 128
TTK = TOK + HALO
NCC = TOK // CHK
NTT = TTK // CHK
DCH = DM // 128
TBS = [(0, 512), (512, 512), (1024, 512), (1536, 512), (2048, 128)]
OCH_XBC = list(range(INTER // 128, (INTER + CONVD) // 128))
OCH_GATE = list(range(0, INTER // 128))
OCH_DT = (INTER + CONVD) // 128

ENGINES = ("sync", "tensor", "vector", "scalar", "gpsimd")
DEBUG = []
DEBUG_XNFM = False
NEG_BIG = -1e38


class Sem:
    def __init__(self, h, name):
        self.h = h
        self.name = name
        self.n = 0


class Builder:
    def __init__(self, nc):
        self.nc = nc
        self.streams = {k: [] for k in ENGINES}
        self._ctx = []

    def sb(self, name, shape, dt):
        t = self.nc.sbuf_tensor(name, shape, dt)
        self._ctx.append((name, t))
        return t.__enter__()

    def psum(self, name, shape, dt):
        t = self.nc.psum_tensor(name, shape, dt)
        self._ctx.append((name, t))
        return t.__enter__()

    def free(self, *names):
        want = list(names)
        for name in want:
            top_name, t = self._ctx[-1]
            assert top_name == name, f"free {name} but stack top is {top_name}"
            t.__exit__(None, None, None)
            self._ctx.pop()

    def sem(self, name):
        s = self.nc.semaphore(name)
        self._ctx.insert(0, (name, s))
        return Sem(s.__enter__(), name)

    def op(self, eng, fn, waits=(), inc=None, inck=1):
        wl = []
        for s, v in waits:
            assert v <= s.n, f"wait {s.name}>={v} but scheduled={s.n}"
            if v > 0:
                wl.append((s.h, v))
        tgt = None
        if inc is not None:
            inc.n += inck
            tgt = (inc.h, inck)

        def run(e):
            for h, v in wl:
                e.wait_ge(h, v)
            ins = fn(e)
            if tgt is not None:
                ins.then_inc(tgt[0], tgt[1])
        self.streams[eng].append(run)
        return inc.n if inc is not None else None

    def set_flush(self, tiles):
        self._flush = tiles

    def opf(self, eng, fn, waits=(), inc=None, inck=1):
        self.op(eng, fn, waits=waits)
        dst, src = self._flush[eng]
        if eng == "scalar":
            self.op(eng, lambda e: e.activation(dst, src, AF.Identity),
                    inc=inc, inck=inck)
        else:
            self.op(eng, lambda e: e.tensor_copy(dst, src), inc=inc,
                    inck=inck)
        return inc.n if inc is not None else None

    def dma(self, queue, out, in_, sem_, waits=(), transpose=False):
        if transpose:
            return self.op(queue, lambda e: e.dma_start_transpose(out, in_),
                           waits=waits, inc=sem_, inck=16)
        return self.op(queue, lambda e: e.dma_start(out=out, in_=in_),
                       waits=waits, inc=sem_, inck=16)

    def finalize(self):
        with self.nc.Block() as blk:
            for name, reg in (("sync", blk.sync), ("tensor", blk.tensor),
                              ("vector", blk.vector), ("scalar", blk.scalar),
                              ("gpsimd", blk.gpsimd)):
                ops = self.streams[name]

                def body(e, ops=ops):
                    for f in ops:
                        f(e)
                reg(body)
        for _, t in reversed(self._ctx):
            t.__exit__(None, None, None)
        self._ctx = []


def bcast8(col_ap, reps, width):
    return col_ap[:, :, None].broadcast_to([128, reps, width])


def r3(flat_ap, reps):
    return flat_ap.rearrange("p (a b) -> p a b", a=reps)


def build():
    nc = bass.Bass("TRN2", target_bir_lowering=False)
    b = Builder(nc)

    ap = {}
    for name, shape in [
        ("x", [TTK, DM]), ("xfm", [DM, TOK]), ("wipT", [PROJ // 128, 128, DCH, 128]),
        ("wopT", [2, DM // 128, 128, DCH, 128]), ("wbT", [DM // 128, 128, DCH, 128]), ("convw", [CONVD, KW]),
        ("convb", [128, CONVD // 128]), ("dtb", [128, 1]), ("dD", [NH]), ("normw", [128, INTER // 128]),
        ("wg9T", [DM, 9]), ("bthr", [1]), ("awT", [DM, 128]),
        ("bwT", [128, DM]), ("bbase", [128, DM // 128]), ("qsel3", [128, 3]),
    ]:
        ap[name] = nc.dram_tensor(name, shape, F32, kind="ExternalInput").ap()
    ap_out = nc.dram_tensor("out", [DM, TOK], F32, kind="ExternalOutput").ap()

    sp_gate = nc.dram_tensor("sp_gate", [INTER, TOK], BF16, kind="Internal").ap()
    sp_xc = nc.dram_tensor("sp_xc", [INTER, TOK], BF16, kind="Internal").ap()
    sp_bc = nc.dram_tensor("sp_bc", [2 * NG * 128, TOK], BF16, kind="Internal").ap()
    sp_csf = nc.dram_tensor("sp_csf", [NCC, NH * CHK], F32, kind="Internal").ap()
    sp_dtf = nc.dram_tensor("sp_dtf", [NCC, NH * CHK], F32, kind="Internal").ap()
    sp_ldtf = nc.dram_tensor("sp_ldtf", [NCC, NH * CHK], F32, kind="Internal").ap()
    sp_ygn = nc.dram_tensor("sp_ygn", [TOK, INTER], BF16, kind="Internal").ap()
    sp_g = nc.dram_tensor("sp_g", [DM, TOK], BF16, kind="Internal").ap()
    sp_part = nc.dram_tensor("sp_part", [DM, TOK], F32, kind="Internal").ap()
    cc_in = nc.dram_tensor("cc_in", [128, INTER + 128], F32, kind="Internal").ap()
    cc_out = nc.dram_tensor("cc_out", [512, INTER + 128], F32, kind="Internal").ap()
    scratchmap = {"sp_gate": sp_gate, "sp_xc": sp_xc, "sp_bc": sp_bc,
                  "sp_csf": sp_csf, "sp_dtf": sp_dtf, "sp_ldtf": sp_ldtf,
                  "sp_ygn": sp_ygn, "sp_g": sp_g, "sp_part": sp_part,
                  "cc_in": cc_in, "cc_out": cc_out}
    dbg_outs = []
    for name in DEBUG:
        src = scratchmap[name]
        o = nc.dram_tensor("dbg_" + name, list(src.tensor.shape), src.dtype,
                           kind="ExternalOutput").ap()
        dbg_outs.append((o, src))

    sPh = b.sem("sPh")
    sCst = b.sem("sCst")
    sX = [b.sem("sX0"), b.sem("sX1")]
    sW = [b.sem("sW0"), b.sem("sW1")]
    sMM = b.sem("sMM")
    sV = b.sem("sV")
    sA = b.sem("sA")
    sG = b.sem("sG")
    sSp = [b.sem("sSp0"), b.sem("sSp1")]
    sSpG = b.sem("sSpG")
    sTrX = [b.sem("sTrX0"), b.sem("sTrX1")]
    sTrG = b.sem("sTrG")
    sTrB = b.sem("sTrB")
    sYL = b.sem("sYL")
    sBn = b.sem("sBn")
    sFl = [b.sem("sFl0"), b.sem("sFl1")]
    sRb = b.sem("sRb")
    sRb2 = b.sem("sRb2")
    sFW = b.sem("sFW")
    sCsl = b.sem("sCsl")
    sPk = b.sem("sPk")
    sCC = b.sem("sCC")
    sOut = b.sem("sOut")

    PH = [0]

    def barrier():
        PH[0] += 1
        tgt = 4 * PH[0]
        for e in ("tensor", "vector", "scalar", "gpsimd"):
            b.op(e, lambda en: en.nop(), inc=sPh)
        for e in ENGINES:
            b.op(e, lambda en: None, waits=[(sPh, tgt)])

    def PW():
        return [(sPh, 4 * PH[0])]

    identf = b.sb("identf", [128, 128], F32)
    identb = b.sb("identb", [128, 128], BF16)
    ones1 = b.sb("ones1", [1, 128], F32)
    ones128 = b.sb("ones128", [128, 128], F32)
    erep = b.sb("erep", [8, 128], F32)
    acol = b.sb("acol", [128, 4], F32)
    acoli = b.sb("acoli", [128, 4], I32)
    convw_sb = b.sb("convw_sb", [128, (CONVD // 128) * KW], F32)
    convb_sb = b.sb("convb_sb", [128, CONVD // 128], F32)
    dtb_sb = b.sb("dtb_sb", [128, 4], F32)
    drow = b.sb("drow", [1, 128], F32)
    dcol_all = b.sb("dcol_all", [128, 128], F32)
    normw_sb = b.sb("normw_sb", [128, INTER // 128], F32)
    bb_sb = b.sb("bb_sb", [128, DM // 128], F32)
    bthr_sb = b.sb("bthr_sb", [1, 1], F32)
    bthr_col = b.sb("bthr_col", [128, 4], F32)
    qsel_sb = b.sb("qsel_sb", [128, 3], F32)
    cconst = b.sb("cconst", [128, 8], F32)
    vfl = b.sb("vfl", [128, 64], F32)
    afl = b.sb("afl", [128, 64], F32)
    maskM = b.sb("maskM", [128, 512], BF16)

    b.set_flush({"vector": (vfl[:, 0:32], vfl[:, 32:64]),
                 "scalar": (afl[:, 0:32], afl[:, 32:64]),
                 "gpsimd": (vfl[:, 0:32], vfl[:, 32:64])})
    pB = b.psum("pB", [128, 512], F32)
    pLn = b.psum("pLn", [128, 128], BF16)
    pT = b.psum("pT", [128, 2560], F32)
    pFt = b.psum("pFt", [128, 128], F32)
    pSS = pB[:, 0:128]

    b.dma("sync", convw_sb[:], ap["convw"].rearrange("(a p) k -> p a k", p=128), sCst)
    b.dma("sync", convb_sb[:], ap["convb"], sCst)
    b.dma("sync", dtb_sb[:, 0:1], ap["dtb"], sCst)
    b.dma("sync", drow[:], ap["dD"].unsqueeze(0), sCst)
    b.dma("sync", normw_sb[:], ap["normw"], sCst)
    b.dma("sync", bb_sb[:], ap["bbase"], sCst)
    b.dma("sync", bthr_sb[:], ap["bthr"].unsqueeze(0), sCst)
    b.dma("sync", qsel_sb[:], ap["qsel3"], sCst)
    CST = sCst.n

    b.op("vector", lambda e: e.memset(vfl[:], 0.0), inc=sV)
    b.op("vector", lambda e: e.memset(afl[:], 0.0), inc=sV)
    b.op("vector", lambda e: e.memset(ones1[:], 1.0), inc=sV)
    b.op("vector", lambda e: e.memset(ones128[:], 1.0), inc=sV)
    for ci, cv in [(0, EPS), (1, 1.0), (2, 1.0 / 1024.0), (3, -0.5),
                   (4, -1.0)]:
        b.op("vector", lambda e, ci=ci, cv=cv: e.memset(
            cconst[:, ci:ci + 1], cv), inc=sV)
    MSET = sV.n
    b.op("gpsimd", lambda e: e.affine_select(
        identf[:], ones128[:], pattern=[[1, 128]], compare_op=ALU.is_equal,
        fill=0.0, base=0, channel_multiplier=-1), waits=[(sV, MSET)], inc=sG)
    IDF = sG.n
    b.op("vector", lambda e: e.tensor_copy(identb[:], identf[:]),
         waits=[(sG, IDF)], inc=sV)
    b.op("gpsimd", lambda e: e.affine_select(
        erep[:], ones128[0:8, :], pattern=[[1, 128]], compare_op=ALU.is_ge,
        fill=0.0, base=0, channel_multiplier=-16), waits=[(sV, MSET)], inc=sG)
    b.op("gpsimd", lambda e: e.affine_select(
        erep[:], erep[:], pattern=[[-1, 128]], compare_op=ALU.is_ge,
        fill=0.0, base=15, channel_multiplier=16), inc=sG)
    b.op("gpsimd", lambda e: e.iota(acoli[:, 0:1], [[0, 1]], base=-1,
                                    channel_multiplier=-1), inc=sG)
    b.op("vector", lambda e: e.memset(maskM[:], 1e38), inc=sV)
    MMSK = sV.n
    b.op("gpsimd", lambda e: e.affine_select(
        maskM[:], maskM[:], pattern=[[0, 4], [1, 128]],
        compare_op=ALU.is_ge, fill=NEG_BIG, base=0,
        channel_multiplier=-1), waits=[(sV, MMSK)], inc=sG)
    GC = sG.n
    b.opf("vector", lambda e: e.tensor_copy(acol[:, 0:1], acoli[:, 0:1]),
          waits=[(sG, GC)], inc=sV)
    b.op("tensor", lambda e: e.matmul(pB[:, 0:128], ones1[:], drow[:],
                                      start=True, stop=False),
         waits=[(sCst, CST), (sV, MSET)])
    b.op("tensor", lambda e: e.matmul(pB[:, 128:129], ones1[:],
                                      bthr_sb[:], start=False, stop=True),
         inc=sMM)
    TC = sMM.n
    b.op("vector", lambda e: e.tensor_copy(dcol_all[:], pB[:, 0:128]),
         waits=[(sMM, TC)], inc=sV)
    b.opf("vector", lambda e: e.tensor_copy(bthr_col[:, 0:1], pB[:, 128:129]),
          inc=sV)

    barrier()

    xnfm = [b.sb(f"xnfm{a}", [128, TTK], BF16) for a in range(DCH)]
    lnx = [b.sb(f"lnx{i}", [128, DM], F32) for i in range(2)]
    lnsq = b.sb("lnsq", [128, DM], F32)
    lncol = b.sb("lncol", [128, 128], F32)
    lnxn = b.sb("lnxn", [128, DM], BF16)

    if DEBUG_XNFM:
        dln = nc.dram_tensor("dbg_lncol", [NTT, 128, 128], F32,
                             kind="ExternalOutput").ap()
        dlx = nc.dram_tensor("dbg_lnx", [NTT, 128, DM], F32,
                             kind="ExternalOutput").ap()
    ln_act = {}
    ln_tr = {}
    lndump = {}
    for i in range(NTT):
        buf = lnx[i % 2]
        w = PW()
        if i >= 2:
            w.append((sA, ln_act[i - 2]))
            if DEBUG_XNFM:
                w.append((sBn, lndump[i - 2]))
        b.dma("sync", buf[:], ap["x"][i * CHK:(i + 1) * CHK, :], sX[i % 2],
              waits=w)
        XL = sX[i % 2].n
        c0 = 64 * (i % 2)
        cS, cQ, cM, cVr, cV2, cR, cR2, cB = (c0, c0 + 4, c0 + 8, c0 + 12,
                                             c0 + 16, c0 + 20, c0 + 24,
                                             c0 + 28)
        b.op("scalar", lambda e, bf=buf, cS=cS: e.activation(
            lnsq[:], bf[:], AF.Identity, accum_out=lncol[:, cS:cS + 1]),
            waits=[(sX[i % 2], XL)], inc=sA)
        b.op("scalar", lambda e, bf=buf, cQ=cQ: e.activation(
            lnsq[:], bf[:], AF.Square, accum_out=lncol[:, cQ:cQ + 1]),
            inc=sA)
        SQ = sA.n
        b.opf("vector", lambda e, cS=cS, cM=cM: e.tensor_scalar(
            lncol[:, cM:cM + 1], lncol[:, cS:cS + 1], 1.0 / DM, None,
            ALU.mult), waits=[(sA, SQ)], inc=sV)
        b.opf("vector", lambda e, cM=cM, cVr=cVr: e.tensor_tensor(
            lncol[:, cVr:cVr + 1], lncol[:, cM:cM + 1], lncol[:, cM:cM + 1],
            ALU.mult), inc=sV)
        b.opf("vector", lambda e, cQ=cQ, cVr=cVr, cV2=cV2:
              e.scalar_tensor_tensor(
                  lncol[:, cV2:cV2 + 1], lncol[:, cQ:cQ + 1], 1.0 / DM,
                  lncol[:, cVr:cVr + 1], ALU.mult, ALU.subtract), inc=sV)
        VV = sV.n
        b.opf("scalar", lambda e, cV2=cV2, cR=cR: e.activation(
            lncol[:, cR:cR + 1], lncol[:, cV2:cV2 + 1], AF.Ln,
            bias=cconst[:, 0:1]), waits=[(sV, VV)], inc=sA)
        b.opf("scalar", lambda e, cR=cR, cR2=cR2: e.activation(
            lncol[:, cR2:cR2 + 1], lncol[:, cR:cR + 1], AF.Exp,
            scale=cconst[:, 3:4]), inc=sA)
        RS = sA.n
        b.opf("vector", lambda e, cM=cM, cR2=cR2, cB=cB:
              e.scalar_tensor_tensor(
                  lncol[:, cB:cB + 1], lncol[:, cM:cM + 1], -1.0,
                  lncol[:, cR2:cR2 + 1], ALU.mult, ALU.mult),
              waits=[(sA, RS)], inc=sV)
        NB = sV.n
        wxn = [(sV, NB)]
        if i >= 1:
            wxn.append((sV, ln_tr[i - 1]))
        b.op("scalar", lambda e, bf=buf, cR2=cR2, cB=cB: e.activation(
            lnxn[:], bf[:], AF.Identity, bias=lncol[:, cB:cB + 1],
            scale=lncol[:, cR2:cR2 + 1]), waits=wxn, inc=sA)
        ln_act[i] = sA.n
        if DEBUG_XNFM:
            b.dma("scalar", dln[i], lncol[:], sBn, waits=[(sA, ln_act[i])])
            b.dma("scalar", dlx[i], buf[:], sBn, waits=[(sA, ln_act[i])])
            lndump[i] = sBn.n
        XN = sA.n
        prev_ev = None
        for a in range(DCH):
            wt = [(sA, XN)]
            if prev_ev is not None:
                wt.append((sV, prev_ev))
            b.op("tensor", lambda e, a=a: e.transpose(
                pLn[:], lnxn[:, a * 128:(a + 1) * 128], identb[:]),
                waits=wt, inc=sMM)
            TR = sMM.n
            b.op("vector", lambda e, a=a, i=i: e.tensor_copy(
                xnfm[a][:, i * CHK:(i + 1) * CHK], pLn[:]),
                waits=[(sMM, TR)], inc=sV)
            prev_ev = sV.n
        ln_tr[i] = sV.n

    barrier()
    b.free("lnxn", "lncol", "lnsq", "lnx1", "lnx0")

    wbuf = [b.sb(f"wbufA{i}", [128, DCH * 128], BF16) for i in range(2)]
    evb = [b.sb(f"evb{i}", [128, TTK], BF16) for i in range(2)]
    convacc = b.sb("convacc", [128, TOK], F32)
    convacc2 = b.sb("convacc2", [128, TOK], F32)

    order = OCH_XBC + OCH_GATE + [OCH_DT]
    mm_ip = {}
    ev_ip = {}
    spill_ip = {}
    eplg = {}
    k_dt = len(order) - 1
    for k, oc in enumerate(order):
        w = PW()
        if k >= 2:
            w.append((sMM, mm_ip[k - 2]))
        b.dma("gpsimd", wbuf[k % 2][:], ap["wipT"][oc], sW[k % 2], waits=w)
        WL = sW[k % 2].n
        for ti, (t0, tl) in enumerate(TBS):
            for a in range(DCH):
                wmm = []
                if a == 0 and ti == 0:
                    wmm = [(sW[k % 2], WL)]
                    if k >= 1:
                        wmm.append((sV, ev_ip[k - 1]))
                b.op("tensor", lambda e, a=a, t0=t0, tl=tl, k=k: e.matmul(
                    pT[:, t0:t0 + tl], wbuf[k % 2][:, a * 128:(a + 1) * 128],
                    xnfm[a][:, t0:t0 + tl], start=(a == 0),
                    stop=(a == DCH - 1)), waits=wmm,
                    inc=(sMM if (a == DCH - 1 and ti == len(TBS) - 1) else None))
        mm_ip[k] = sMM.n
        wev = [(sMM, mm_ip[k])]
        if k >= 2:
            wev.append(eplg[k - 2])
            if (k - 2) in spill_ip:
                wev.append((sSp[k % 2], spill_ip[k - 2]))
        b.op("vector", lambda e, k=k: e.tensor_copy(evb[k % 2][:],
                                                    pT[:, 0:TTK]),
             waits=wev, inc=sV)
        ev_ip[k] = sV.n
        if 64 <= oc <= 143:
            ac = oc - 64
            cacc = [convacc, convacc2]
            b.op("vector", lambda e, k=k, ac=ac: e.tensor_scalar(
                cacc[0][:], evb[k % 2][:, 125:125 + TOK],
                convw_sb[:, ac * KW:ac * KW + 1], None, ALU.mult),
                waits=[(sV, ev_ip[k])], inc=sV)
            for kk in range(1, KW):
                b.op("vector", lambda e, k=k, ac=ac, kk=kk:
                     e.scalar_tensor_tensor(
                         cacc[kk % 2][:],
                         evb[k % 2][:, 125 + kk:125 + kk + TOK],
                         convw_sb[:, ac * KW + kk:ac * KW + kk + 1],
                         cacc[(kk + 1) % 2][:], ALU.mult, ALU.add),
                     inc=(sV if kk == KW - 1 else None))
            CNV = sV.n
            wsil = [(sV, CNV)]
            if k >= 2 and (k - 2) in spill_ip:
                wsil.append((sSp[k % 2], spill_ip[k - 2]))
            b.op("scalar", lambda e, k=k, ac=ac: e.activation(
                evb[k % 2][:, 0:TOK], convacc2[:], AF.Silu,
                bias=convb_sb[:, ac:ac + 1]), waits=wsil, inc=sA)
            eplg[k] = (sA, sA.n)
            dst = sp_xc[(oc - 64) * 128:(oc - 63) * 128, :] if oc <= 127 \
                else sp_bc[(oc - 128) * 128:(oc - 127) * 128, :]
            b.dma("scalar", dst, evb[k % 2][:, 0:TOK], sSp[k % 2],
                  waits=[eplg[k]])
            spill_ip[k] = sSp[k % 2].n
        elif oc == OCH_DT:
            eplg[k] = (sV, ev_ip[k])
        else:
            wsil = [(sV, ev_ip[k])]
            if k >= 2 and (k - 2) in spill_ip:
                wsil.append((sSp[k % 2], spill_ip[k - 2]))
            b.op("scalar", lambda e, k=k: e.activation(
                evb[k % 2][:, HALO:TTK], evb[k % 2][:, HALO:TTK], AF.Silu),
                waits=wsil, inc=sA)
            eplg[k] = (sA, sA.n)
            b.dma("scalar", sp_gate[oc * 128:(oc + 1) * 128, :],
                  evb[k % 2][:, HALO:TTK], sSp[k % 2], waits=[eplg[k]])
            spill_ip[k] = sSp[k % 2].n

    dtproc = b.sb("dtproc", [128, TOK], F32)
    lndt = b.sb("lndt", [128, TOK], F32)
    cst = b.sb("cst", [128, TOK], F32)
    dtev = evb[k_dt % 2]
    dtw = convacc

    b.op("vector", lambda e: e.tensor_scalar(
        dtproc[:], dtev[:, HALO:TTK], dtb_sb[:, 0:1], 0.0, ALU.add, ALU.max),
        waits=[(sV, ev_ip[k_dt]), (sG, sG.n)], inc=sV)
    b.op("scalar", lambda e: e.activation(
        dtw[:], dtev[:, HALO:TTK], AF.Abs, bias=dtb_sb[:, 0:1]),
        waits=[(sV, ev_ip[k_dt]), (sG, sG.n)], inc=sA)
    b.op("scalar", lambda e: e.activation(dtw[:], dtw[:], AF.Exp,
         scale=cconst[:, 4:5]), inc=sA)
    b.op("scalar", lambda e: e.activation(dtw[:], dtw[:], AF.Ln,
         bias=cconst[:, 1:2]), inc=sA)
    DTL = sA.n
    b.op("vector", lambda e: e.tensor_tensor(dtproc[:], dtproc[:], dtw[:],
                                             ALU.add), waits=[(sA, DTL)],
         inc=sV)
    b.op("vector", lambda e: e.tensor_scalar(
        dtproc[:], dtproc[:], 1e-3, 100.0, ALU.max, ALU.min), inc=sV)
    DTP = sV.n
    b.op("scalar", lambda e: e.activation(lndt[:], dtproc[:], AF.Ln),
         waits=[(sV, DTP)], inc=sA)
    LND = sA.n
    b.op("vector", lambda e: e.tensor_scalar(
        dtw[:], dtproc[:], acol[:, 0:1], None, ALU.mult), waits=[(sA, LND)],
        inc=sV)
    DA = sV.n
    for c in range(NCC):
        b.op("vector", lambda e, c=c: e.tensor_tensor_scan(
            cst[:, c * CHK:(c + 1) * CHK], ones128[:],
            dtw[:, c * CHK:(c + 1) * CHK], 0.0, ALU.mult, ALU.add),
            waits=[(sV, DA)], inc=sV)
    CS = sV.n
    for c in range(NCC):
        b.dma("scalar", sp_csf[c:c + 1, :].rearrange("o (h t) -> (o h) t", h=128),
              cst[:, c * CHK:(c + 1) * CHK], sBn, waits=[(sV, CS)])
        b.dma("scalar", sp_dtf[c:c + 1, :].rearrange("o (h t) -> (o h) t", h=128),
              dtproc[:, c * CHK:(c + 1) * CHK], sBn, waits=[(sV, DTP)])
        b.dma("scalar", sp_ldtf[c:c + 1, :].rearrange("o (h t) -> (o h) t", h=128),
              lndt[:, c * CHK:(c + 1) * CHK], sBn, waits=[(sA, LND)])
    FLATS = sBn.n
    if DEBUG_XNFM:
        dxn = nc.dram_tensor("dbg_xnfm", [DCH * 128, TTK], BF16,
                             kind="ExternalOutput").ap()
        for a in range(DCH):
            b.dma("sync", dxn[a * 128:(a + 1) * 128, :], xnfm[a][:], sBn,
                  waits=[(sMM, mm_ip[k_dt])])
        FLATS = sBn.n
    b.op("vector", lambda e: None,
         waits=[(sBn, FLATS), (sSp[0], sSp[0].n), (sSp[1], sSp[1].n)])
    barrier()
    b.free("cst", "lndt", "dtproc", "convacc2", "convacc", "evb1", "evb0",
           "wbufA1", "wbufA0")

    wbufF = [b.sb(f"wbufF{i}", [128, DCH * 128], BF16) for i in range(2)]
    wg9sb = b.sb("wg9sb", [128, DCH * 9], BF16)
    bwT_sb = b.sb("bwT_sb", [128, DM], BF16)
    h_sb = b.sb("h_sb", [128, TOK], BF16)
    hw_sb = b.sb("hw_sb", [128, TOK], BF16)
    glsb = b.sb("glsb", [9, 512], F32)
    gltm = b.sb("gltm", [128, 9], F32)
    gcol = b.sb("gcol", [128, 32], F32)
    wnsb = b.sb("wnsb", [128, 8], F32)
    wnsb2 = b.sb("wnsb2", [128, 8], F32)
    wTst = b.sb("wTst", [8, 512], F32)
    gstage = [b.sb(f"gstage{i}", [128, TOK], BF16) for i in range(2)]

    b.dma("gpsimd", wg9sb[:], ap["wg9T"].rearrange("(a p) j -> p a j", p=128),
          sFW, waits=PW())
    b.dma("gpsimd", bwT_sb[:], ap["bwT"], sFW)
    b.dma("gpsimd", wbufF[0][:], ap["awT"].rearrange("(a p) r -> p a r", p=128),
          sFW)
    FW = sFW.n

    for ti in range(4):
        for a in range(DCH):
            b.op("tensor", lambda e, a=a, ti=ti: e.matmul(
                pT[:, ti * 512:(ti + 1) * 512],
                wbufF[0][:, a * 128:(a + 1) * 128],
                xnfm[a][:, HALO + ti * 512:HALO + (ti + 1) * 512],
                start=(a == 0), stop=(a == DCH - 1)),
                waits=([(sFW, FW)] + PW()) if (a == 0 and ti == 0) else [],
                inc=(sMM if (a == DCH - 1 and ti == 3) else None))
    HMM = sMM.n
    b.op("vector", lambda e: e.tensor_copy(h_sb[:], pT[:, 0:TOK]),
         waits=[(sMM, HMM)], inc=sV)
    HEV = sV.n

    for ti in range(4):
        for a in range(DCH):
            b.op("tensor", lambda e, a=a, ti=ti: e.matmul(
                pT[0:9, ti * 512:(ti + 1) * 512],
                wg9sb[:, a * 9:(a + 1) * 9],
                xnfm[a][:, HALO + ti * 512:HALO + (ti + 1) * 512],
                start=(a == 0), stop=(a == DCH - 1)),
                waits=[(sFW, FW), (sV, HEV)] if (a == 0 and ti == 0) else [],
                inc=(sMM if (a == DCH - 1 and ti == 3) else None))
    GL = sMM.n
    hw_mark = {}
    for ti in range(4):
        wev = [(sMM, GL)]
        b.op("vector", lambda e, ti=ti: e.tensor_copy(
            glsb[:], pT[0:9, ti * 512:(ti + 1) * 512]), waits=wev, inc=sV)
        GEV = sV.n
        for tc in range(4):
            b.op("tensor", lambda e, tc=tc: e.transpose(
                pFt[:, 0:9], glsb[:, tc * 128:(tc + 1) * 128],
                identf[0:9, 0:9]), waits=[(sV, GEV), (sV, sV.n)], inc=sMM)
            b.opf("vector", lambda e: e.tensor_copy(gltm[:], pFt[:, 0:9]),
                  waits=[(sMM, sMM.n)], inc=sV)
            b.opf("vector", lambda e: e.reduce_max(
                gcol[:, 0:1], gltm[:, 0:8], axis=AX.X), inc=sV)
            b.opf("vector", lambda e: e.tensor_scalar(
                gcol[:, 4:5], gcol[:, 0:1], -1.0, None, ALU.mult), inc=sV)
            NM = sV.n
            b.op("scalar", lambda e: e.activation(
                gltm[:, 0:8], gltm[:, 0:8], AF.Exp, bias=gcol[:, 4:5]),
                waits=[(sV, NM)], inc=sA)
            EX = sA.n
            b.opf("vector", lambda e: e.reduce_sum(
                gcol[:, 8:9], gltm[:, 0:8], axis=AX.X), waits=[(sA, EX)],
                inc=sV)
            b.opf("vector", lambda e: e.reciprocal(gcol[:, 12:13],
                                                   gcol[:, 8:9]), inc=sV)
            b.opf("scalar", lambda e: e.activation(
                gcol[:, 16:17], gltm[:, 8:9], AF.Sigmoid,
                bias=bthr_col[:, 0:1]), inc=sA)
            TH = sA.n
            b.opf("vector", lambda e: e.tensor_scalar(
                gcol[:, 28:29], gcol[:, 16:17], 0.125, None, ALU.mult),
                waits=[(sA, TH)], inc=sV)
            b.opf("vector", lambda e: e.tensor_scalar(
                wnsb[:], gltm[:, 0:8], gcol[:, 12:13], gcol[:, 28:29],
                ALU.mult, ALU.subtract), inc=sV)
            b.opf("vector", lambda e: e.tensor_scalar(
                wnsb2[:], wnsb[:], 0.0, None, ALU.max), inc=sV)
            b.opf("vector", lambda e: e.reduce_sum(
                gcol[:, 20:21], wnsb2[:], axis=AX.X), inc=sV)
            b.opf("vector", lambda e: e.tensor_scalar(
                gcol[:, 24:25], gcol[:, 20:21], 0.0, None, ALU.is_equal),
                inc=sV)
            b.opf("vector", lambda e: e.tensor_tensor(
                gcol[:, 20:21], gcol[:, 20:21], gcol[:, 24:25], ALU.add),
                inc=sV)
            b.opf("vector", lambda e: e.reciprocal(gcol[:, 12:13],
                                                   gcol[:, 20:21]), inc=sV)
            b.opf("vector", lambda e: e.tensor_scalar(
                wnsb[:], wnsb2[:], gcol[:, 12:13], None, ALU.mult), inc=sV)
            WN = sV.n
            b.op("tensor", lambda e, tc=tc: e.transpose(
                pFt[0:8, :], wnsb[:], identf[:]), waits=[(sV, WN)], inc=sMM)
            b.op("vector", lambda e, tc=tc: e.tensor_copy(
                wTst[:, tc * 128:(tc + 1) * 128], pFt[0:8, 0:128]),
                waits=[(sMM, sMM.n)], inc=sV)
        WT = sV.n
        wwbc = [(sV, WT)]
        if ti > 0:
            wwbc.append((sV, hw_mark[ti - 1]))
        b.op("tensor", lambda e, ti=ti: e.matmul(
            pB[:, 0:512], erep[:], wTst[:], start=True, stop=True),
            waits=wwbc, inc=sMM)
        WB = sMM.n
        b.op("vector", lambda e, ti=ti: e.scalar_tensor_tensor(
            hw_sb[:, ti * 512:(ti + 1) * 512],
            h_sb[:, ti * 512:(ti + 1) * 512], 2.0, pB[:, 0:512],
            ALU.mult, ALU.mult), waits=[(sMM, WB)], inc=sV)
        hw_mark[ti] = sV.n
    HW = sV.n

    mm_f = {}
    ev_f = {}
    for j in range(DCH):
        w = [(sV, HW)]
        if j >= 2:
            w.append((sMM, mm_f[j - 2]))
        b.dma("gpsimd", wbufF[j % 2][:], ap["wbT"][j], sW[j % 2], waits=w)
        WL = sW[j % 2].n
        for ti in range(4):
            for a in range(DCH):
                wmm = []
                if a == 0 and ti == 0:
                    wmm = [(sW[j % 2], WL)]
                    if j >= 1:
                        wmm.append((sA, ev_f[j - 1]))
                b.op("tensor", lambda e, a=a, ti=ti, j=j: e.matmul(
                    pT[:, ti * 512:(ti + 1) * 512],
                    wbufF[j % 2][:, a * 128:(a + 1) * 128],
                    xnfm[a][:, HALO + ti * 512:HALO + (ti + 1) * 512],
                    start=(a == 0), stop=False), waits=wmm)
        for ti in range(4):
            b.op("tensor", lambda e, ti=ti, j=j: e.matmul(
                pT[:, ti * 512:(ti + 1) * 512],
                bwT_sb[:, j * 128:(j + 1) * 128],
                hw_sb[:, ti * 512:(ti + 1) * 512], start=False, stop=True),
                inc=(sMM if ti == 3 else None))
        mm_f[j] = sMM.n
        wg = [(sMM, mm_f[j])]
        if j >= 2:
            wg.append((sSp[j % 2], spill_ip[("f", j - 2)]))
        b.op("scalar", lambda e, j=j: e.activation(
            gstage[j % 2][:], pT[:, 0:TOK], AF.Silu, bias=bb_sb[:, j:j + 1]),
            waits=wg, inc=sA)
        ev_f[j] = sA.n
        b.dma("scalar", sp_g[j * 128:(j + 1) * 128, :], gstage[j % 2][:],
              sSp[j % 2], waits=[(sA, ev_f[j])])
        spill_ip[("f", j)] = sSp[j % 2].n

    b.op("vector", lambda e: None, waits=[(sSp[0], sSp[0].n),
                                          (sSp[1], sSp[1].n)])
    barrier()
    b.free("gstage1", "gstage0", "wTst", "wnsb2", "wnsb", "gcol", "gltm", "glsb",
           "hw_sb", "h_sb", "bwT_sb", "wg9sb", "wbufF1", "wbufF0")
    for a in reversed(range(DCH)):
        b.free(f"xnfm{a}")

    csT_all = b.sb("csT_all", [128, TOK], F32)
    ddtT_all = b.sb("ddtT_all", [128, TOK], F32)
    cdcol_all = b.sb("cdcol_all", [128, TOK], F32)
    cdtot = b.sb("cdtot", [128, 128], F32)
    carry = b.sb("carry", [128, INTER], F32)
    btm = b.sb("btm", [128, NCC * NG * 128], BF16)
    xtm = [b.sb(f"xtm{i}", [128, INTER], BF16) for i in range(2)]
    xdd = [b.sb(f"xdd{i}", [128, 512], BF16) for i in range(2)]
    sctmp = [b.sb(f"sctmp{i}", [128, 512], F32) for i in range(2)]
    csw = b.sb("csw", [128, 128], F32)
    dtwn = b.sb("dtwn", [128, 128], F32)
    cslr = b.sb("cslr", [1, 128], F32)
    cdrow = b.sb("cdrow", [1, 128], F32)
    dcyT = b.sb("dcyT", [128, 128], F32)

    b.op("vector", lambda e: e.memset(carry[:], 0.0), waits=PW(), inc=sV)
    b.op("vector", lambda e: e.memset(cdtot[:], 1.0), inc=sV)
    INITA = sV.n

    for c in range(NCC):
        b.dma("sync", btm[:, c * NG * 128:(c + 1) * NG * 128],
              sp_bc[0:NG * 128, c * CHK:(c + 1) * CHK], sTrB,
              waits=PW() if c == 0 else (), transpose=True)
    BTM = sTrB.n

    mkA = {}
    for c in range(NCC):
        wrb = PW()
        if c >= 1:
            wrb.append((sV, mkA[("ddt", c - 1)]))
        b.dma("sync", csw[:], sp_csf[c:c + 1, :]
              .rearrange("o (h t) -> (o h) t", h=128), sRb, waits=wrb)
        b.dma("sync", dtwn[:], sp_dtf[c:c + 1, :]
              .rearrange("o (h t) -> (o h) t", h=128), sRb)
        RB = sRb.n
        wtr = [(sRb, RB)]
        if c >= 1:
            wtr.append((sV, mkA[("cslr", c - 1)]))
        b.op("tensor", lambda e: e.transpose(pFt[:], csw[:], identf[:]),
             waits=wtr, inc=sMM)
        b.op("vector", lambda e, c=c: e.tensor_copy(
            csT_all[:, c * CHK:(c + 1) * CHK], pFt[:]),
            waits=[(sMM, sMM.n)], inc=sV)
        CST = sV.n
        b.op("tensor", lambda e: e.transpose(pFt[:], dtwn[:], identf[:]),
             waits=[(sV, CST)], inc=sMM)
        b.op("vector", lambda e: e.tensor_copy(dcyT[:], pFt[:]),
             waits=[(sMM, sMM.n)], inc=sV)
        mkA[("dtT", c)] = sV.n
        wcsl = [(sRb, RB), (sV, sV.n)]
        b.op("tensor", lambda e: e.transpose(
            pFt[0:1, 0:128], csw[:, 127:128], identf[:]), waits=wcsl,
            inc=sMM)
        CRT = sMM.n
        b.opf("vector", lambda e: e.tensor_copy(cslr[:], pFt[0:1, 0:128]),
              waits=[(sMM, CRT)], inc=sV)
        CR = sV.n
        mkA[("cslr", c)] = CR
        b.opf("scalar", lambda e: e.activation(cdrow[:], cslr[:], AF.Exp),
              waits=[(sV, CR)], inc=sA)
        CD = sA.n
        wbc0 = [(sA, CD)]
        if c >= 1:
            wbc0.append((sV, mkA[("ddt", c - 1)]))
        b.op("tensor", lambda e: e.matmul(pB[:, 0:128], ones1[:], cdrow[:],
                                          start=True, stop=False),
             waits=wbc0)
        b.op("tensor", lambda e: e.matmul(pB[:, 128:256], ones1[:], cslr[:],
                                          start=False, stop=True), inc=sMM)
        BC = sMM.n
        mkA[("bc", c)] = BC
        b.op("vector", lambda e, c=c: e.tensor_copy(
            cdcol_all[:, c * CHK:(c + 1) * CHK], pB[:, 0:128]),
            waits=[(sMM, BC)], inc=sV)
        b.op("vector", lambda e, c=c: e.tensor_tensor(
            ddtT_all[:, c * CHK:(c + 1) * CHK], pB[:, 128:256],
            csT_all[:, c * CHK:(c + 1) * CHK], ALU.subtract), inc=sV)
        DS = sV.n
        b.op("scalar", lambda e, c=c: e.activation(
            ddtT_all[:, c * CHK:(c + 1) * CHK],
            ddtT_all[:, c * CHK:(c + 1) * CHK], AF.Exp), waits=[(sV, DS)],
            inc=sA)
        DE = sA.n
        b.op("vector", lambda e, c=c: e.tensor_tensor(
            ddtT_all[:, c * CHK:(c + 1) * CHK],
            ddtT_all[:, c * CHK:(c + 1) * CHK], dcyT[:], ALU.mult),
            waits=[(sA, DE)], inc=sV)
        mkA[("ddt", c)] = sV.n
        DDT = sV.n
        b.op("vector", lambda e, c=c: e.tensor_tensor(
            cdtot[:], cdtot[:], cdcol_all[:, c * CHK:(c + 1) * CHK],
            ALU.mult), inc=sV)
        wx = PW()
        if c >= 2:
            wx.append((sMM, mkA[("stmm", c - 2, 15)]))
        b.dma("sync", xtm[c % 2][:], sp_xc[:, c * CHK:(c + 1) * CHK],
              sTrX[c % 2], waits=wx, transpose=True)
        XT = sTrX[c % 2].n

        for hb in range(16):
            h0 = hb * 8
            g = h0 // 16
            bank = pT[:, (hb % 2) * 512:(hb % 2) * 512 + 512]
            wxd = [(sTrX[c % 2], XT), (sV, DDT)]
            prev_stmm = mkA.get(("stmm", c, hb - 2),
                                mkA.get(("stmm", c - 1, 14 + hb)))
            if prev_stmm is not None:
                wxd.append((sMM, prev_stmm))
            b.op("vector", lambda e, c=c, h0=h0, hb=hb: e.tensor_tensor(
                r3(xdd[hb % 2][:], 8),
                r3(xtm[c % 2][:, h0 * HD:(h0 + 8) * HD], 8),
                bcast8(ddtT_all[:, c * CHK + h0:c * CHK + h0 + 8], 8, HD),
                ALU.mult), waits=wxd, inc=sV)
            XD = sV.n
            wst = [(sV, XD), (sTrB, BTM)]
            prev_scan = mkA.get(("scan", c, hb - 2),
                                mkA.get(("scan", c - 1, 14 + hb)))
            if prev_scan is not None:
                wst.append((sV, prev_scan))
            b.op("tensor", lambda e, c=c, g=g, hb=hb, bank=bank: e.matmul(
                bank, btm[:, c * NG * 128 + g * 128:c * NG * 128 + (g + 1) * 128],
                xdd[hb % 2][:], start=True, stop=True), waits=wst, inc=sMM)
            mkA[("stmm", c, hb)] = sMM.n
            wsc = [(sV, INITA)]
            if c >= 1:
                wsc = [(sV, mkA[("scan", c - 1, hb)])]
            prev_any = mkA.get(("scan", c, hb - 2),
                               mkA.get(("scan", c - 1, 14 + hb)))
            if prev_any is not None:
                wsc.append((sV, prev_any))
            b.op("vector", lambda e, c=c, h0=h0, hb=hb: e.tensor_tensor(
                r3(sctmp[hb % 2][:], 8),
                r3(carry[:, h0 * HD:(h0 + 8) * HD], 8),
                bcast8(cdcol_all[:, c * CHK + h0:c * CHK + h0 + 8], 8, HD),
                ALU.mult), waits=wsc, inc=sV)
            SC = sV.n
            b.op("vector", lambda e, h0=h0, hb=hb, bank=bank: e.tensor_tensor(
                carry[:, h0 * HD:(h0 + 8) * HD], sctmp[hb % 2][:], bank,
                ALU.add), waits=[(sV, SC), (sMM, mkA[("stmm", c, hb)])],
                inc=sV)
            mkA[("scan", c, hb)] = sV.n

    SCANF = sV.n
    b.dma("scalar", cc_in[:, 0:INTER], carry[:], sPk, waits=[(sV, SCANF)])
    b.dma("scalar", cc_in[:, INTER:INTER + 128], cdtot[:], sPk)
    PK = sPk.n
    b.op("gpsimd", lambda e: e.collective_compute(
        "AllGather", ALU.bypass, replica_groups=[[0, 1, 2, 3], [4, 5, 6, 7]],
        ins=[cc_in.opt()], outs=[cc_out.opt()]), waits=[(sPk, PK)], inc=sCC)
    CCD = sCC.n

    cd12 = b.sb("cd12", [128, 256], F32)
    ctmp = b.sb("ctmp", [128, TOK], F32)
    Fst = b.sb("Fst", [128, TOK], F32)
    Ast = b.sb("Ast", [128, TOK], F32)
    b.dma("sync", cd12[:, 0:128], cc_out[128:256, INTER:INTER + 128], sRb2,
          waits=[(sCC, CCD)])
    b.dma("sync", cd12[:, 128:256], cc_out[256:384, INTER:INTER + 128], sRb2)
    CD12 = sRb2.n
    for s4 in range(4):
        col = s4 * TOK
        h0 = s4 * 32
        b.dma("sync", Ast[:], cc_out[0:128, col:col + TOK], sRb2,
              waits=[(sV, sV.n), (sCC, CCD)])
        AL = sRb2.n
        b.op("vector", lambda e, col=col: e.tensor_scalar(
            carry[:, col:col + TOK], Ast[:], qsel_sb[:, 0:1], None,
            ALU.mult), waits=[(sRb2, AL)], inc=sV)
        for j in (1, 2):
            b.dma("sync", Fst[:], cc_out[j * 128:(j + 1) * 128, col:col + TOK],
                  sRb2, waits=[(sV, sV.n)])
            FL = sRb2.n
            b.op("vector", lambda e, j=j, h0=h0: e.tensor_tensor(
                r3(Ast[:], 32), r3(Ast[:], 32),
                bcast8(cd12[:, (j - 1) * 128 + h0:(j - 1) * 128 + h0 + 32],
                       32, HD), ALU.mult),
                waits=[(sRb2, FL), (sRb2, CD12)], inc=sV)
            b.op("vector", lambda e: e.tensor_tensor(
                Ast[:], Ast[:], Fst[:], ALU.add), inc=sV)
            b.op("vector", lambda e, j=j: e.tensor_scalar(
                ctmp[:], Ast[:], qsel_sb[:, j:j + 1], None, ALU.mult),
                inc=sV)
            b.op("vector", lambda e, col=col: e.tensor_tensor(
                carry[:, col:col + TOK], carry[:, col:col + TOK], ctmp[:],
                ALU.add), inc=sV)
    CINIT = sV.n
    barrier()
    b.free("Ast", "Fst", "ctmp", "cd12", "dcyT", "cdrow", "cslr", "dtwn", "csw")

    cslndtT = b.sb("cslndtT", [128, 128], F32)
    expw = b.sb("expw", [128, 128], F32)
    S_g = b.sb("S_g", [128, 128], F32)
    bcfm = b.sb("bcfm", [128, 2 * NG * 128], BF16)
    flat4 = [b.sb(f"flat4_{i}", [1, 512], F32) for i in range(2)]
    segb = [b.sb(f"segb{i}", [128, 512], F32) for i in range(2)]
    m4g = [b.sb(f"m4g{i}", [128, 2048], BF16) for i in range(2)]
    dx8 = [b.sb(f"dx8_{i}", [128, 512], BF16) for i in range(2)]
    cv8 = [b.sb(f"cv8_{i}", [128, 512], BF16) for i in range(2)]
    gtm = b.sb("gtm", [128, INTER], BF16)
    Yt = b.sb("Yt", [128, INTER], F32)
    rmsc = b.sb("rmsc", [128, 64], F32)

    mk = {}
    ydg_hist = {0: [], 1: []}
    seg_hist = []
    csbc_hist = []
    fl_hist = {0: [], 1: []}
    batch_hist = []

    for c in range(NCC):
        wrb = PW()
        if c >= 1:
            wrb.append((sMM, mk[("ltr", c - 1)]))
            wrb.append((sV, seg_hist[-1]))
        b.dma("sync", segb[0][:, 0:128], sp_ldtf[c:c + 1, :]
              .rearrange("o (h t) -> (o h) t", h=128), sRb, waits=wrb)
        LR = sRb.n
        wltr = [(sRb, LR)]
        if c >= 1:
            wltr.append((sV, mk[("pc", c - 1)]))
        b.op("tensor", lambda e: e.transpose(pFt[:], segb[0][:, 0:128],
                                             identf[:]), waits=wltr, inc=sMM)
        mk[("ltr", c)] = sMM.n
        b.op("vector", lambda e, c=c: e.scalar_tensor_tensor(
            cslndtT[:], pFt[:], -1.0, csT_all[:, c * CHK:(c + 1) * CHK],
            ALU.mult, ALU.add), waits=[(sMM, mk[("ltr", c)])], inc=sV)
        mk[("pc", c)] = sV.n
        CSL = sV.n
        wex = [(sV, CSL)]
        if c >= 1:
            wex.append((sV, mk[("mrgall", c - 1)]))
        b.op("scalar", lambda e, c=c: e.activation(
            expw[:], csT_all[:, c * CHK:(c + 1) * CHK], AF.Exp),
            waits=wex, inc=sA)
        EXW = sA.n
        wx = PW()
        if c >= 2:
            wx.append((sV, mk[("ygdone", c - 2)]))
        b.dma("sync", xtm[c % 2][:], sp_xc[:, c * CHK:(c + 1) * CHK],
              sTrX[c % 2], waits=wx, transpose=True)
        XT = sTrX[c % 2].n
        wg = PW()
        if c >= 1:
            wg.append((sV, mk[("ygdone", c - 1)]))
        b.dma("sync", gtm[:], sp_gate[:, c * CHK:(c + 1) * CHK], sTrG,
              waits=wg, transpose=True)
        GT = sTrG.n
        wbc = PW()
        if c >= 1:
            wbc.append((sMM, mk[("Slast", c - 1)]))
        b.dma("sync", bcfm[:], sp_bc[:, c * CHK:(c + 1) * CHK]
              .rearrange("(u n) t -> n u t", n=128), sRb2, waits=wbc)
        BCL = sRb2.n

        for g in range(NG):
            wS = [(sRb2, BCL)]
            if (c, g) != (0, 0):
                wS.append((sV, mk.get(("Sev", c, g - 1),
                                      mk.get(("Sev", c - 1, NG - 1)))))
            b.op("tensor", lambda e, g=g: e.matmul(
                pSS, bcfm[:, g * 128:(g + 1) * 128],
                bcfm[:, (NG + g) * 128:(NG + g + 1) * 128],
                start=True, stop=True), waits=wS, inc=sMM)
            SM = sMM.n
            if g == NG - 1:
                mk[("Slast", c)] = sMM.n
            wsev = [(sMM, SM)]
            if seg_hist:
                wsev.append((sV, seg_hist[-1]))
            b.op("vector", lambda e: e.tensor_copy(S_g[:], pSS),
                 waits=wsev, inc=sV)
            mk[("Sev", c, g)] = sV.n
            SEV = sV.n

            for q4 in range(4):
                h0 = g * 16 + q4 * 4
                pb = q4 % 2
                wfl = []
                if len(fl_hist[pb]) >= 1:
                    wfl.append((sMM, fl_hist[pb][-1]))
                b.dma("sync", flat4[pb][:],
                      sp_csf[c:c + 1, h0 * CHK:h0 * CHK + 512], sFl[pb],
                      waits=wfl)
                FLQ = sFl[pb].n
                wcb = [(sFl[pb], FLQ), (sV, MSET)]
                if len(csbc_hist) >= 2:
                    wcb.append((sV, csbc_hist[-2]))
                b.op("tensor", lambda e, pb=pb: e.matmul(
                    pT[:, pb * 512:pb * 512 + 512], ones1[:], flat4[pb][:],
                    start=True, stop=True), waits=wcb, inc=sMM)
                CBM = sMM.n
                fl_hist[pb].append(CBM)
                wsg = [(sMM, CBM), (sV, mk[("pc", c)])]
                if len(seg_hist) >= 2:
                    wsg.append((sV, seg_hist[-2]))
                b.op("vector", lambda e, pb=pb, h0=h0: e.tensor_tensor(
                    segb[pb][:].rearrange("p (a t) -> p a t", a=4),
                    pT[:, pb * 512:pb * 512 + 512]
                    .rearrange("p (a t) -> p a t", a=4),
                    cslndtT[:, h0:h0 + 4][:, :, None]
                    .broadcast_to([128, 4, 128]), ALU.subtract),
                    waits=wsg, inc=sV)
                SGT = sV.n
                csbc_hist.append(SGT)
                b.op("vector", lambda e, pb=pb: e.tensor_tensor(
                    segb[pb][:], segb[pb][:], maskM[:], ALU.min),
                    waits=[(sV, SGT)], inc=sV)
                MS = sV.n
                b.op("scalar", lambda e, pb=pb: e.activation(
                    segb[pb][:], segb[pb][:], AF.Exp), waits=[(sV, MS)],
                    inc=sA)
                EXP = sA.n
                wm4 = [(sA, EXP), (sV, SEV)]
                if ydg_hist[g % 2]:
                    wm4.append((sMM, ydg_hist[g % 2][-1]))
                b.op("vector", lambda e, pb=pb, g=g, q4=q4: e.tensor_tensor(
                    m4g[g % 2][:, q4 * 512:(q4 + 1) * 512]
                    .rearrange("p (a t) -> p a t", a=4),
                    segb[pb][:].rearrange("p (a t) -> p a t", a=4),
                    S_g[:, None, :].broadcast_to([128, 4, 128]), ALU.mult),
                    waits=wm4, inc=sV)
                mk[("m4", c, g, q4)] = sV.n
                seg_hist.append(sV.n)
            mk[("m4last", c)] = sV.n

            for hb in range(2):
                h0 = g * 16 + hb * 8
                prev = batch_hist[-1] if batch_hist else None
                wxd = [(sTrX[c % 2], XT), (sV, mkA[("ddt", c)])]
                if prev is not None:
                    wxd.append((sMM, prev["st"]))
                b.op("vector", lambda e, c=c, h0=h0, hb=hb: e.tensor_tensor(
                    r3(xdd[hb][:], 8),
                    r3(xtm[c % 2][:, h0 * HD:(h0 + 8) * HD], 8),
                    bcast8(ddtT_all[:, c * CHK + h0:c * CHK + h0 + 8], 8, HD),
                    ALU.mult), waits=wxd, inc=sV)
                wdx = [(sTrX[c % 2], XT)]
                if prev is not None:
                    wdx.append((sMM, prev["yd"]))
                b.op("vector", lambda e, c=c, h0=h0, hb=hb: e.tensor_tensor(
                    r3(dx8[hb][:], 8),
                    r3(xtm[c % 2][:, h0 * HD:(h0 + 8) * HD], 8),
                    bcast8(dcol_all[:, h0:h0 + 8], 8, HD), ALU.mult),
                    waits=wdx, inc=sV)
                wcv = [(sV, mk[("scan", c - 1, g, hb)] if c >= 1 else CINIT)]
                if prev is not None:
                    wcv.append((sMM, prev["yo"]))
                XDD5 = sV.n
                b.op("gpsimd", lambda e, h0=h0, hb=hb: e.tensor_copy(
                    cv8[hb][:], carry[:, h0 * HD:(h0 + 8) * HD]),
                    waits=wcv, inc=sG)
                GXD = sG.n
                wst = [(sV, XDD5), (sTrB, BTM)]
                if prev is not None:
                    wst.append((sV, prev["scan"]))
                b.op("tensor", lambda e, c=c, g=g, hb=hb: e.matmul(
                    pT[:, 2048:2560],
                    btm[:, c * NG * 128 + g * 128:c * NG * 128 + (g + 1) * 128],
                    xdd[hb][:], start=True, stop=True), waits=wst, inc=sMM)
                STM = sMM.n
                wyd = [(sV, XDD5)]
                if prev is not None:
                    wyd.append((sV, prev["mrg2"]))
                b.op("tensor", lambda e, hb=hb: e.matmul(
                    pT[:, 1024:1536], identb[:], dx8[hb][:], start=True,
                    stop=False), waits=wyd)
                for hh in range(8):
                    ig = hb * 8 + hh
                    b.op("tensor", lambda e, c=c, g=g, ig=ig, hh=hh: e.matmul(
                        pT[:, 1024 + hh * HD:1024 + (hh + 1) * HD],
                        m4g[g % 2][:, ig * 128:(ig + 1) * 128],
                        xtm[c % 2][:, (g * 16 + ig) * HD:
                                    (g * 16 + ig + 1) * HD],
                        start=False, stop=(hh == 7)),
                        waits=([(sV, mk[("m4", c, g, ig // 4)])]
                               if hh % 4 == 0 else []),
                        inc=(sMM if hh == 7 else None))
                YDM = sMM.n
                if hb == 1:
                    ydg_hist[g % 2].append(YDM)
                wyo = [(sG, GXD)]
                if prev is not None:
                    wyo.append((sV, prev["mrg1"]))
                for hh in range(8):
                    b.op("tensor", lambda e, g=g, hb=hb, hh=hh: e.matmul(
                        pT[:, 1536 + hh * HD:1536 + (hh + 1) * HD],
                        bcfm[:, (NG + g) * 128:(NG + g + 1) * 128],
                        cv8[hb][:, hh * HD:(hh + 1) * HD],
                        start=(hh == 0), stop=(hh == 7)),
                        waits=(wyo if hh == 0 else ()),
                        inc=(sMM if hh == 7 else None))
                YOM = sMM.n
                wsc2 = [(sG, GXD)]
                if prev is not None:
                    wsc2.append((sV, prev["scan"]))
                    wsc2.append((sV, prev["mrg2"]))
                b.op("vector", lambda e, c=c, h0=h0, hb=hb: e.tensor_tensor(
                    r3(sctmp[hb][:], 8),
                    r3(carry[:, h0 * HD:(h0 + 8) * HD], 8),
                    bcast8(cdcol_all[:, c * CHK + h0:c * CHK + h0 + 8], 8, HD),
                    ALU.mult), waits=wsc2, inc=sV)
                SC2 = sV.n
                b.op("vector", lambda e, h0=h0, hb=hb: e.tensor_tensor(
                    carry[:, h0 * HD:(h0 + 8) * HD], sctmp[hb][:],
                    pT[:, 2048:2560], ALU.add),
                    waits=[(sV, SC2), (sMM, STM), (sG, GXD)], inc=sV)
                mk[("scan", c, g, hb)] = sV.n
                SCN = sV.n
                b.op("vector", lambda e, h0=h0, hb=hb: e.tensor_tensor(
                    r3(sctmp[1 - hb][:], 8),
                    pT[:, 1536:2048].rearrange("p (a b) -> p a b", a=8),
                    expw[:, h0:h0 + 8][:, :, None].broadcast_to([128, 8, HD]),
                    ALU.mult), waits=[(sMM, YOM), (sA, EXW)], inc=sV)
                MR1 = sV.n
                wmr2 = [(sMM, YDM)]
                if c >= 1 and (g, hb) == (0, 0):
                    wmr2.append((sSpG, mk[("spill", c - 1)]))
                b.op("vector", lambda e, h0=h0, hb=hb: e.tensor_tensor(
                    Yt[:, h0 * HD:(h0 + 8) * HD], pT[:, 1024:1536],
                    sctmp[1 - hb][:], ALU.add), waits=wmr2, inc=sV)
                MR2 = sV.n
                batch_hist.append({"st": STM, "yd": YDM, "yo": YOM,
                                   "scan": SCN, "mrg1": MR1, "mrg2": MR2})
        mk[("mrgall", c)] = sV.n

        YD = mk[("mrgall", c)]
        b.op("vector", lambda e: e.tensor_tensor(Yt[:], Yt[:], gtm[:],
                                                 ALU.mult),
             waits=[(sV, YD), (sTrG, GT)], inc=sV)
        YG = sV.n
        for gr in range(8):
            b.op("scalar", lambda e, gr=gr: e.activation(
                gtm[:, 0:1024], Yt[:, gr * 1024:(gr + 1) * 1024], AF.Square,
                accum_out=rmsc[:, 4 * gr:4 * gr + 1]), waits=[(sV, YG)], inc=sA)
        SQA = sA.n
        b.opf("scalar", lambda e: e.activation(
            rmsc[:, 32:64].rearrange("p (a u) -> p a u", u=4)[:, :, 0:1],
            rmsc[:, 0:32].rearrange("p (a u) -> p a u", u=4)[:, :, 0:1],
            AF.Ln, bias=cconst[:, 0:1], scale=cconst[:, 2:3]),
            waits=[(sA, SQA)], inc=sA)
        b.opf("scalar", lambda e: e.activation(
            rmsc[:, 32:64].rearrange("p (a u) -> p a u", u=4)[:, :, 0:1],
            rmsc[:, 32:64].rearrange("p (a u) -> p a u", u=4)[:, :, 0:1],
            AF.Exp, scale=cconst[:, 3:4]), inc=sA)
        RSD = sA.n
        for gr in range(8):
            b.op("vector", lambda e, gr=gr: e.tensor_scalar(
                Yt[:, gr * 1024:(gr + 1) * 1024],
                Yt[:, gr * 1024:(gr + 1) * 1024], rmsc[:, 32 + 4 * gr:33 + 4 * gr],
                None, ALU.mult), waits=[(sA, RSD)], inc=sV)
        mk[("ygdone", c)] = sV.n
        YGN = sV.n
        wsp = [(sV, YGN)]
        if c >= 1:
            wsp.append((sSpG, mk[("spill", c - 1)]))
        b.dma("gpsimd", sp_ygn[c * CHK:(c + 1) * CHK, :], Yt[:], sSpG,
              waits=wsp)
        mk[("spill", c)] = sSpG.n

    b.op("vector", lambda e: None, waits=[(sSpG, sSpG.n)])
    barrier()
    b.free("rmsc", "Yt", "gtm", "cv8_1", "cv8_0", "dx8_1",
           "dx8_0", "m4g1", "m4g0", "segb1", "segb0", "flat4_1", "flat4_0",
           "bcfm", "S_g", "expw", "cslndtT")
    b.free("sctmp1", "sctmp0", "xdd1", "xdd0", "xtm1", "xtm0", "btm", "carry",
           "cdtot", "cdcol_all", "ddtT_all", "csT_all")

    wbufO = [b.sb(f"wbufO{i}", [128, DCH * 128], BF16) for i in range(2)]
    ytl = [b.sb(f"ytl{a}", [128, TOK], BF16) for a in range(DCH)]
    prtb = [b.sb(f"prtb{i}", [128, TOK], F32) for i in range(2)]
    grb = b.sb("grb", [128, TOK], BF16)
    xrb = b.sb("xrb", [128, TOK], F32)
    ssb = b.sb("ssb", [128, TOK], F32)
    osb = [b.sb(f"osb{i}", [128, TOK], F32) for i in range(2)]

    mm_op = {}
    ev_op = {}
    for p in range(2):
        for a in range(DCH):
            wl = PW()
            if p == 1:
                wl.append((sMM, mm_op[(0, DCH - 1)]))
                wl.append((sV, ev_op[(0, DCH - 1)]))
            b.dma("sync", ytl[a][:],
                  sp_ygn[:, p * DM + a * 128:p * DM + (a + 1) * 128], sYL,
                  waits=(wl if a == 0 else ()), transpose=True)
        YLD = sYL.n
        for a in range(DCH):
            b.op("vector", lambda e, a=a, p=p: e.tensor_scalar(
                ytl[a][:], ytl[a][:],
                normw_sb[:, p * DCH + a:p * DCH + a + 1], None, ALU.mult),
                waits=[(sYL, YLD)], inc=sV)
        NWS = sV.n
        for j in range(DCH):
            w = [(sV, NWS)]
            if j >= 2:
                w.append((sMM, mm_op[(p, j - 2)]))
            elif p == 1:
                w.append((sMM, mm_op[(0, DCH - 2 + j)]))
            b.dma("gpsimd", wbufO[j % 2][:], ap["wopT"][p, j], sW[j % 2],
                  waits=w)
            WL = sW[j % 2].n
            for ti in range(4):
                for a in range(DCH):
                    wmm = []
                    if a == 0 and ti == 0:
                        wmm = [(sW[j % 2], WL)]
                        pe = ev_op.get((p, j - 1),
                                       ev_op.get((0, DCH - 1)) if p == 1 else None)
                        if pe is not None:
                            wmm.append((sV, pe))
                    b.op("tensor", lambda e, a=a, ti=ti, j=j: e.matmul(
                        pT[:, ti * 512:(ti + 1) * 512],
                        wbufO[j % 2][:, a * 128:(a + 1) * 128],
                        ytl[a][:, ti * 512:(ti + 1) * 512],
                        start=(a == 0), stop=(a == DCH - 1)), waits=wmm,
                        inc=(sMM if (a == DCH - 1 and ti == 3) else None))
            mm_op[(p, j)] = sMM.n
            if p == 0:
                wev = [(sMM, mm_op[(p, j)])]
                if j >= 2:
                    wev.append((sSp[j % 2], mk[("psp", j - 2)]))
                b.op("vector", lambda e, j=j: e.tensor_copy(
                    prtb[j % 2][:], pT[:, 0:TOK]), waits=wev, inc=sV)
                ev_op[(p, j)] = sV.n
                b.dma("scalar", sp_part[j * 128:(j + 1) * 128, :],
                      prtb[j % 2][:], sSp[j % 2], waits=[(sV, ev_op[(p, j)])])
                mk[("psp", j)] = sSp[j % 2].n
            else:
                wrb = [(sV, ev_op.get((1, j - 1), NWS))]
                if j == 0:
                    wrb += [(sSp[0], sSp[0].n), (sSp[1], sSp[1].n)]
                b.dma("sync", prtb[j % 2][:],
                      sp_part[j * 128:(j + 1) * 128, :], sRb, waits=wrb)
                b.dma("sync", grb[:], sp_g[j * 128:(j + 1) * 128, :], sRb)
                b.dma("sync", xrb[:], ap["xfm"][j * 128:(j + 1) * 128, :],
                      sRb)
                PR = sRb.n
                wfin = [(sMM, mm_op[(p, j)]), (sRb, PR)]
                if j >= 2:
                    wfin.append((sOut, mk[("out", j - 2)]))
                b.op("vector", lambda e, j=j: e.tensor_tensor(
                    ssb[:], pT[:, 0:TOK], prtb[j % 2][:], ALU.add),
                    waits=wfin, inc=sV)
                b.op("vector", lambda e: e.tensor_tensor(
                    ssb[:], ssb[:], grb[:], ALU.mult), inc=sV)
                b.op("vector", lambda e, j=j: e.tensor_tensor(
                    osb[j % 2][:], ssb[:], xrb[:], ALU.add), inc=sV)
                ev_op[(p, j)] = sV.n
                b.dma("scalar", ap_out[j * 128:(j + 1) * 128, :],
                      osb[j % 2][:], sOut, waits=[(sV, ev_op[(p, j)])])
                mk[("out", j)] = sOut.n

    for o, src in dbg_outs:
        b.dma("sync", o, src, sOut,
              waits=[(sOut, sOut.n), (sSpG, sSpG.n), (sSp[0], sSp[0].n),
                     (sSp[1], sSp[1].n), (sBn, sBn.n)])
    b.op("scalar", lambda e: None, waits=[(sOut, sOut.n)])

    b.finalize()
    return nc


def prep_inputs(inputs):
    g = {k: np.asarray(v) for k, v in inputs.items()}
    x = np.ascontiguousarray(g["L0_attn_hidden_states"], dtype=np.float32)
    B, S, D = x.shape
    xf = x.reshape(B * S, D)

    _wip = g["L0_attn_in_proj_w"].T.astype(np.float32)
    wipT = np.ascontiguousarray(
        _wip.reshape(DCH, 128, PROJ // 128, 128).transpose(2, 1, 0, 3))
    _wop = g["L0_attn_out_proj_w"].T.astype(np.float32)
    wopT = np.ascontiguousarray(
        _wop.reshape(2, DCH, 128, DM // 128, 128).transpose(0, 3, 2, 1, 4))
    _wb = g["L0_ffn_W_base"].T.astype(np.float32)
    wbT = np.ascontiguousarray(
        _wb.reshape(DCH, 128, DM // 128, 128).transpose(2, 1, 0, 3))
    convw = np.ascontiguousarray(
        g["L0_attn_conv_w"].reshape(CONVD, KW).astype(np.float32))
    convb = np.ascontiguousarray(g["L0_attn_conv_b"].astype(np.float32).reshape(CONVD // 128, 128).T)
    dtb = np.ascontiguousarray(g["L0_attn_dt_bias"].astype(np.float32).reshape(128, 1))
    dD = np.ascontiguousarray(g["L0_attn_D"].astype(np.float32))
    normw = np.ascontiguousarray(g["L0_attn_norm_w"].astype(np.float32).reshape(INTER // 128, 128).T)
    wg9T = np.ascontiguousarray(np.concatenate(
        [g["L0_ffn_W_gate"].T, g["L0_ffn_W_thr"][:, None]],
        axis=1).astype(np.float32))
    bthr = np.ascontiguousarray(g["L0_ffn_b_thr"].astype(np.float32))
    awT = np.ascontiguousarray(
        g["L0_ffn_A_w"].reshape(128, DM).T.astype(np.float32))
    bwT = np.ascontiguousarray(
        g["L0_ffn_B_w"].transpose(0, 2, 1).reshape(128, DM).astype(np.float32))
    bbase = np.ascontiguousarray(g["L0_ffn_b_base"].astype(np.float32).reshape(DM // 128, 128).T)

    in_maps = []
    for core in range(8):
        seq, q = core // 4, core % 4
        t0 = seq * S + q * TOK
        own = xf[t0:t0 + TOK]
        halo = np.zeros((HALO, D), np.float32) if q == 0 else \
            xf[t0 - HALO:t0]
        xc = np.ascontiguousarray(np.concatenate([halo, own], axis=0))
        xfm = np.ascontiguousarray(own.T)
        qsel = np.zeros((128, 3), np.float32)
        if q >= 1:
            qsel[:, q - 1] = 1.0
        in_maps.append({
            "x": xc, "xfm": xfm, "wipT": wipT, "wopT": wopT, "wbT": wbT,
            "convw": convw, "convb": convb, "dtb": dtb, "dD": dD,
            "normw": normw, "wg9T": wg9T, "bthr": bthr, "awT": awT,
            "bwT": bwT, "bbase": bbase, "qsel3": qsel,
        })
    return in_maps, (B, S, D)


def kernel(**inputs):
    in_maps, (B, S, D) = prep_inputs(inputs)
    nc = build()
    res = run_bass_kernel_spmd(nc, in_maps, core_ids=list(range(8)))
    outs = [np.asarray(r["out"]).T for r in res.results]
    full = np.concatenate(outs, axis=0).reshape(B, S, D)
    return full.astype(np.float32)


# version 10 (final)
# speedup vs baseline: 1.0088x; 1.0088x over previous
import numpy as np
import concourse.bass as bass
import concourse.mybir as mybir
from concourse.bass_utils import run_bass_kernel_spmd

F32 = mybir.dt.float32
BF16 = mybir.dt.bfloat16
I32 = mybir.dt.int32
AF = mybir.ActivationFunctionType
ALU = mybir.AluOpType
AX = mybir.AxisListType

DM = 4096
NH = 128
HD = 64
NG = 8
CHK = 128
INTER = NH * HD
CONVD = INTER + 2 * NG * 128
PROJ = INTER + CONVD + NH
KW = 4
EPS = 1e-5

TOK = 2048
HALO = 128
TTK = TOK + HALO
NCC = TOK // CHK
NTT = TTK // CHK
DCH = DM // 128
TBS = [(0, 512), (512, 512), (1024, 512), (1536, 512), (2048, 128)]
OCH_XBC = list(range(INTER // 128, (INTER + CONVD) // 128))
OCH_GATE = list(range(0, INTER // 128))
OCH_DT = (INTER + CONVD) // 128

ENGINES = ("sync", "tensor", "vector", "scalar", "gpsimd")
DEBUG = []
DEBUG_XNFM = False
NEG_BIG = -1e38


class Sem:
    def __init__(self, h, name):
        self.h = h
        self.name = name
        self.n = 0


class Builder:
    def __init__(self, nc):
        self.nc = nc
        self.streams = {k: [] for k in ENGINES}
        self._ctx = []

    def sb(self, name, shape, dt):
        t = self.nc.sbuf_tensor(name, shape, dt)
        self._ctx.append((name, t))
        return t.__enter__()

    def psum(self, name, shape, dt):
        t = self.nc.psum_tensor(name, shape, dt)
        self._ctx.append((name, t))
        return t.__enter__()

    def free(self, *names):
        want = list(names)
        for name in want:
            top_name, t = self._ctx[-1]
            assert top_name == name, f"free {name} but stack top is {top_name}"
            t.__exit__(None, None, None)
            self._ctx.pop()

    def sem(self, name):
        s = self.nc.semaphore(name)
        self._ctx.insert(0, (name, s))
        return Sem(s.__enter__(), name)

    def op(self, eng, fn, waits=(), inc=None, inck=1):
        wl = []
        for s, v in waits:
            assert v <= s.n, f"wait {s.name}>={v} but scheduled={s.n}"
            if v > 0:
                wl.append((s.h, v))
        tgt = None
        if inc is not None:
            inc.n += inck
            tgt = (inc.h, inck)

        def run(e):
            for h, v in wl:
                e.wait_ge(h, v)
            ins = fn(e)
            if tgt is not None:
                ins.then_inc(tgt[0], tgt[1])
        self.streams[eng].append(run)
        return inc.n if inc is not None else None

    def set_flush(self, tiles):
        self._flush = tiles

    def opf(self, eng, fn, waits=(), inc=None, inck=1):
        self.op(eng, fn, waits=waits)
        dst, src = self._flush[eng]
        if eng == "scalar":
            self.op(eng, lambda e: e.activation(dst, src, AF.Identity),
                    inc=inc, inck=inck)
        else:
            self.op(eng, lambda e: e.tensor_copy(dst, src), inc=inc,
                    inck=inck)
        return inc.n if inc is not None else None

    def dma(self, queue, out, in_, sem_, waits=(), transpose=False):
        if transpose:
            return self.op(queue, lambda e: e.dma_start_transpose(out, in_),
                           waits=waits, inc=sem_, inck=16)
        return self.op(queue, lambda e: e.dma_start(out=out, in_=in_),
                       waits=waits, inc=sem_, inck=16)

    def finalize(self):
        with self.nc.Block() as blk:
            for name, reg in (("sync", blk.sync), ("tensor", blk.tensor),
                              ("vector", blk.vector), ("scalar", blk.scalar),
                              ("gpsimd", blk.gpsimd)):
                ops = self.streams[name]

                def body(e, ops=ops):
                    for f in ops:
                        f(e)
                reg(body)
        for _, t in reversed(self._ctx):
            t.__exit__(None, None, None)
        self._ctx = []


def bcast8(col_ap, reps, width):
    return col_ap[:, :, None].broadcast_to([128, reps, width])


def r3(flat_ap, reps):
    return flat_ap.rearrange("p (a b) -> p a b", a=reps)


def build():
    nc = bass.Bass("TRN2", target_bir_lowering=False)
    b = Builder(nc)

    ap = {}
    for name, shape in [
        ("x", [TTK, DM]), ("xfm", [DM, TOK]), ("wipT", [PROJ // 128, 128, DCH, 128]),
        ("wopT", [2, DM // 128, 128, DCH, 128]), ("wbT", [DM // 128, 128, DCH, 128]), ("convw", [CONVD, KW]),
        ("convb", [128, CONVD // 128]), ("dtb", [128, 1]), ("dD", [NH]), ("normw", [128, INTER // 128]),
        ("wg9T", [DM, 9]), ("bthr", [1]), ("awT", [DM, 128]),
        ("bwT", [128, DM]), ("bbase", [128, DM // 128]), ("qsel3", [128, 3]),
    ]:
        ap[name] = nc.dram_tensor(name, shape, F32, kind="ExternalInput").ap()
    ap_out = nc.dram_tensor("out", [DM, TOK], F32, kind="ExternalOutput").ap()

    sp_gate = nc.dram_tensor("sp_gate", [INTER, TOK], BF16, kind="Internal").ap()
    sp_xc = nc.dram_tensor("sp_xc", [INTER, TOK], BF16, kind="Internal").ap()
    sp_bc = nc.dram_tensor("sp_bc", [2 * NG * 128, TOK], BF16, kind="Internal").ap()
    sp_csf = nc.dram_tensor("sp_csf", [NCC, NH * CHK], F32, kind="Internal").ap()
    sp_dtf = nc.dram_tensor("sp_dtf", [NCC, NH * CHK], F32, kind="Internal").ap()
    sp_ldtf = nc.dram_tensor("sp_ldtf", [NCC, NH * CHK], F32, kind="Internal").ap()
    sp_ygn = nc.dram_tensor("sp_ygn", [TOK, INTER], BF16, kind="Internal").ap()
    sp_g = nc.dram_tensor("sp_g", [DM, TOK], BF16, kind="Internal").ap()
    sp_part = nc.dram_tensor("sp_part", [DM, TOK], F32, kind="Internal").ap()
    cc_in = nc.dram_tensor("cc_in", [128, INTER + 128], F32, kind="Internal").ap()
    cc_out = nc.dram_tensor("cc_out", [512, INTER + 128], F32, kind="Internal").ap()
    scratchmap = {"sp_gate": sp_gate, "sp_xc": sp_xc, "sp_bc": sp_bc,
                  "sp_csf": sp_csf, "sp_dtf": sp_dtf, "sp_ldtf": sp_ldtf,
                  "sp_ygn": sp_ygn, "sp_g": sp_g, "sp_part": sp_part,
                  "cc_in": cc_in, "cc_out": cc_out}
    dbg_outs = []
    for name in DEBUG:
        src = scratchmap[name]
        o = nc.dram_tensor("dbg_" + name, list(src.tensor.shape), src.dtype,
                           kind="ExternalOutput").ap()
        dbg_outs.append((o, src))

    sPh = b.sem("sPh")
    sCst = b.sem("sCst")
    sX = [b.sem("sX0"), b.sem("sX1")]
    sW = [b.sem("sW0"), b.sem("sW1"), b.sem("sW2")]
    sMM = b.sem("sMM")
    sV = b.sem("sV")
    sA = b.sem("sA")
    sG = b.sem("sG")
    sSp = [b.sem("sSp0"), b.sem("sSp1")]
    sSpG = b.sem("sSpG")
    sTrX = [b.sem("sTrX0"), b.sem("sTrX1")]
    sTrG = b.sem("sTrG")
    sTrB = b.sem("sTrB")
    sYL = b.sem("sYL")
    sBn = b.sem("sBn")
    sFl = [b.sem("sFl0"), b.sem("sFl1")]
    sRb = b.sem("sRb")
    sRb2 = b.sem("sRb2")
    sFW = b.sem("sFW")
    sCsl = b.sem("sCsl")
    sLd = b.sem("sLd")
    sPk = b.sem("sPk")
    sCC = b.sem("sCC")
    sOut = b.sem("sOut")

    PH = [0]

    def barrier():
        PH[0] += 1
        tgt = 4 * PH[0]
        for e in ("tensor", "vector", "scalar", "gpsimd"):
            b.op(e, lambda en: en.nop(), inc=sPh)
        for e in ENGINES:
            b.op(e, lambda en: None, waits=[(sPh, tgt)])

    def PW():
        return [(sPh, 4 * PH[0])]

    identf = b.sb("identf", [128, 128], F32)
    identb = b.sb("identb", [128, 128], BF16)
    ones1 = b.sb("ones1", [1, 128], F32)
    ones128 = b.sb("ones128", [128, 128], F32)
    erep = b.sb("erep", [8, 128], F32)
    acol = b.sb("acol", [128, 4], F32)
    acoli = b.sb("acoli", [128, 4], I32)
    convw_sb = b.sb("convw_sb", [128, (CONVD // 128) * KW], F32)
    convb_sb = b.sb("convb_sb", [128, CONVD // 128], F32)
    dtb_sb = b.sb("dtb_sb", [128, 4], F32)
    drow = b.sb("drow", [1, 128], F32)
    dcol_all = b.sb("dcol_all", [128, 128], F32)
    normw_sb = b.sb("normw_sb", [128, INTER // 128], F32)
    bb_sb = b.sb("bb_sb", [128, DM // 128], F32)
    bthr_sb = b.sb("bthr_sb", [1, 1], F32)
    bthr_col = b.sb("bthr_col", [128, 4], F32)
    qsel_sb = b.sb("qsel_sb", [128, 3], F32)
    cconst = b.sb("cconst", [128, 8], F32)
    vfl = b.sb("vfl", [128, 32], F32)
    afl = b.sb("afl", [128, 32], F32)
    maskM = b.sb("maskM", [128, 512], BF16)

    b.set_flush({"vector": (vfl[:, 0:16], vfl[:, 16:32]),
                 "scalar": (afl[:, 0:16], afl[:, 16:32]),
                 "gpsimd": (vfl[:, 0:16], vfl[:, 16:32])})
    pB = b.psum("pB", [128, 512], F32)
    pLn = b.psum("pLn", [128, 128], BF16)
    pT = b.psum("pT", [128, 2560], F32)
    pFt = b.psum("pFt", [128, 128], F32)
    pSS = pB[:, 0:128]

    b.dma("sync", convw_sb[:], ap["convw"].rearrange("(a p) k -> p a k", p=128), sCst)
    b.dma("sync", convb_sb[:], ap["convb"], sCst)
    b.dma("sync", dtb_sb[:, 0:1], ap["dtb"], sCst)
    b.dma("sync", drow[:], ap["dD"].unsqueeze(0), sCst)
    b.dma("sync", normw_sb[:], ap["normw"], sCst)
    b.dma("sync", bb_sb[:], ap["bbase"], sCst)
    b.dma("sync", bthr_sb[:], ap["bthr"].unsqueeze(0), sCst)
    b.dma("sync", qsel_sb[:], ap["qsel3"], sCst)
    CST = sCst.n

    b.op("vector", lambda e: e.memset(vfl[:], 0.0), inc=sV)
    b.op("vector", lambda e: e.memset(afl[:], 0.0), inc=sV)
    b.op("vector", lambda e: e.memset(ones1[:], 1.0), inc=sV)
    b.op("vector", lambda e: e.memset(ones128[:], 1.0), inc=sV)
    for ci, cv in [(0, EPS), (1, 1.0), (2, 1.0 / 1024.0), (3, -0.5),
                   (4, -1.0)]:
        b.op("vector", lambda e, ci=ci, cv=cv: e.memset(
            cconst[:, ci:ci + 1], cv), inc=sV)
    MSET = sV.n
    b.op("gpsimd", lambda e: e.affine_select(
        identf[:], ones128[:], pattern=[[1, 128]], compare_op=ALU.is_equal,
        fill=0.0, base=0, channel_multiplier=-1), waits=[(sV, MSET)], inc=sG)
    IDF = sG.n
    b.op("vector", lambda e: e.tensor_copy(identb[:], identf[:]),
         waits=[(sG, IDF)], inc=sV)
    b.op("gpsimd", lambda e: e.affine_select(
        erep[:], ones128[0:8, :], pattern=[[1, 128]], compare_op=ALU.is_ge,
        fill=0.0, base=0, channel_multiplier=-16), waits=[(sV, MSET)], inc=sG)
    b.op("gpsimd", lambda e: e.affine_select(
        erep[:], erep[:], pattern=[[-1, 128]], compare_op=ALU.is_ge,
        fill=0.0, base=15, channel_multiplier=16), inc=sG)
    b.op("gpsimd", lambda e: e.iota(acoli[:, 0:1], [[0, 1]], base=-1,
                                    channel_multiplier=-1), inc=sG)
    b.op("vector", lambda e: e.memset(maskM[:], 1e38), inc=sV)
    MMSK = sV.n
    b.op("gpsimd", lambda e: e.affine_select(
        maskM[:], maskM[:], pattern=[[0, 4], [1, 128]],
        compare_op=ALU.is_ge, fill=NEG_BIG, base=0,
        channel_multiplier=-1), waits=[(sV, MMSK)], inc=sG)
    GC = sG.n
    b.opf("vector", lambda e: e.tensor_copy(acol[:, 0:1], acoli[:, 0:1]),
          waits=[(sG, GC)], inc=sV)
    b.op("tensor", lambda e: e.matmul(pB[:, 0:128], ones1[:], drow[:],
                                      start=True, stop=False),
         waits=[(sCst, CST), (sV, MSET)])
    b.op("tensor", lambda e: e.matmul(pB[:, 128:129], ones1[:],
                                      bthr_sb[:], start=False, stop=True),
         inc=sMM)
    TC = sMM.n
    b.op("vector", lambda e: e.tensor_copy(dcol_all[:], pB[:, 0:128]),
         waits=[(sMM, TC)], inc=sV)
    b.opf("vector", lambda e: e.tensor_copy(bthr_col[:, 0:1], pB[:, 128:129]),
          inc=sV)

    barrier()

    xnfm = [b.sb(f"xnfm{a}", [128, TTK], BF16) for a in range(DCH)]
    lnx = [b.sb(f"lnx{i}", [128, DM], F32) for i in range(2)]
    lnsq = b.sb("lnsq", [128, DM], F32)
    lncol = b.sb("lncol", [128, 64], F32)
    lnxn = b.sb("lnxn", [128, DM], BF16)

    if DEBUG_XNFM:
        dln = nc.dram_tensor("dbg_lncol", [NTT, 128, 128], F32,
                             kind="ExternalOutput").ap()
        dlx = nc.dram_tensor("dbg_lnx", [NTT, 128, DM], F32,
                             kind="ExternalOutput").ap()
    ln_act = {}
    ln_tr = {}
    lndump = {}
    for i in range(NTT):
        buf = lnx[i % 2]
        w = PW()
        if i >= 2:
            w.append((sA, ln_act[i - 2]))
            if DEBUG_XNFM:
                w.append((sBn, lndump[i - 2]))
        b.dma("sync", buf[:], ap["x"][i * CHK:(i + 1) * CHK, :], sX[i % 2],
              waits=w)
        XL = sX[i % 2].n
        c0 = 32 * (i % 2)
        cS, cQ, cM, cVr, cV2, cR, cR2, cB = (c0, c0 + 4, c0 + 8, c0 + 12,
                                             c0 + 16, c0 + 20, c0 + 24,
                                             c0 + 28)
        b.op("scalar", lambda e, bf=buf, cS=cS: e.activation(
            lnsq[:], bf[:], AF.Identity, accum_out=lncol[:, cS:cS + 1]),
            waits=[(sX[i % 2], XL)], inc=sA)
        b.op("scalar", lambda e, bf=buf, cQ=cQ: e.activation(
            lnsq[:], bf[:], AF.Square, accum_out=lncol[:, cQ:cQ + 1]),
            inc=sA)
        SQ = sA.n
        b.opf("vector", lambda e, cS=cS, cM=cM: e.tensor_scalar(
            lncol[:, cM:cM + 1], lncol[:, cS:cS + 1], 1.0 / DM, None,
            ALU.mult), waits=[(sA, SQ)], inc=sV)
        b.opf("vector", lambda e, cM=cM, cVr=cVr: e.tensor_tensor(
            lncol[:, cVr:cVr + 1], lncol[:, cM:cM + 1], lncol[:, cM:cM + 1],
            ALU.mult), inc=sV)
        b.opf("vector", lambda e, cQ=cQ, cVr=cVr, cV2=cV2:
              e.scalar_tensor_tensor(
                  lncol[:, cV2:cV2 + 1], lncol[:, cQ:cQ + 1], 1.0 / DM,
                  lncol[:, cVr:cVr + 1], ALU.mult, ALU.subtract), inc=sV)
        VV = sV.n
        b.opf("scalar", lambda e, cV2=cV2, cR=cR: e.activation(
            lncol[:, cR:cR + 1], lncol[:, cV2:cV2 + 1], AF.Ln,
            bias=cconst[:, 0:1]), waits=[(sV, VV)], inc=sA)
        b.opf("scalar", lambda e, cR=cR, cR2=cR2: e.activation(
            lncol[:, cR2:cR2 + 1], lncol[:, cR:cR + 1], AF.Exp,
            scale=cconst[:, 3:4]), inc=sA)
        RS = sA.n
        b.opf("vector", lambda e, cM=cM, cR2=cR2, cB=cB:
              e.scalar_tensor_tensor(
                  lncol[:, cB:cB + 1], lncol[:, cM:cM + 1], -1.0,
                  lncol[:, cR2:cR2 + 1], ALU.mult, ALU.mult),
              waits=[(sA, RS)], inc=sV)
        NB = sV.n
        wxn = [(sV, NB)]
        if i >= 1:
            wxn.append((sV, ln_tr[i - 1]))
        b.op("scalar", lambda e, bf=buf, cR2=cR2, cB=cB: e.activation(
            lnxn[:], bf[:], AF.Identity, bias=lncol[:, cB:cB + 1],
            scale=lncol[:, cR2:cR2 + 1]), waits=wxn, inc=sA)
        ln_act[i] = sA.n
        if DEBUG_XNFM:
            b.dma("scalar", dln[i], lncol[:], sBn, waits=[(sA, ln_act[i])])
            b.dma("scalar", dlx[i], buf[:], sBn, waits=[(sA, ln_act[i])])
            lndump[i] = sBn.n
        XN = sA.n
        prev_ev = None
        for a in range(DCH):
            wt = [(sA, XN)]
            if prev_ev is not None:
                wt.append((sV, prev_ev))
            b.op("tensor", lambda e, a=a: e.transpose(
                pLn[:], lnxn[:, a * 128:(a + 1) * 128], identb[:]),
                waits=wt, inc=sMM)
            TR = sMM.n
            b.op("vector", lambda e, a=a, i=i: e.tensor_copy(
                xnfm[a][:, i * CHK:(i + 1) * CHK], pLn[:]),
                waits=[(sMM, TR)], inc=sV)
            prev_ev = sV.n
        ln_tr[i] = sV.n

    barrier()
    b.free("lnxn", "lncol", "lnsq", "lnx1", "lnx0")

    wbuf = [b.sb(f"wbufA{i}", [128, DCH * 128], BF16) for i in range(3)]
    evb = [b.sb(f"evb{i}", [128, TTK], BF16) for i in range(2)]
    convacc = b.sb("convacc", [128, TOK], F32)
    convacc2 = b.sb("convacc2", [128, TOK], F32)

    order = OCH_XBC + OCH_GATE + [OCH_DT]
    mm_ip = {}
    ev_ip = {}
    spill_ip = {}
    eplg = {}
    k_dt = len(order) - 1
    for k, oc in enumerate(order):
        w = PW()
        if k >= 3:
            w.append((sMM, mm_ip[k - 3]))
        b.dma("gpsimd", wbuf[k % 3][:], ap["wipT"][oc], sW[k % 3], waits=w)
        WL = sW[k % 3].n
        for ti, (t0, tl) in enumerate(TBS):
            for a in range(DCH):
                wmm = []
                if a == 0 and ti == 0:
                    wmm = [(sW[k % 3], WL)]
                    if k >= 1:
                        wmm.append((sV, ev_ip[k - 1]))
                b.op("tensor", lambda e, a=a, t0=t0, tl=tl, k=k: e.matmul(
                    pT[:, t0:t0 + tl], wbuf[k % 3][:, a * 128:(a + 1) * 128],
                    xnfm[a][:, t0:t0 + tl], start=(a == 0),
                    stop=(a == DCH - 1)), waits=wmm,
                    inc=(sMM if (a == DCH - 1 and ti == len(TBS) - 1) else None))
        mm_ip[k] = sMM.n
        wev = [(sMM, mm_ip[k])]
        if k >= 2:
            wev.append(eplg[k - 2])
            if (k - 2) in spill_ip:
                wev.append((sSp[k % 2], spill_ip[k - 2]))
        b.op("vector", lambda e, k=k: e.tensor_copy(evb[k % 2][:],
                                                    pT[:, 0:TTK]),
             waits=wev, inc=sV)
        ev_ip[k] = sV.n
        if 64 <= oc <= 143:
            ac = oc - 64
            cacc = [convacc, convacc2]
            b.op("vector", lambda e, k=k, ac=ac: e.tensor_scalar(
                cacc[0][:], evb[k % 2][:, 125:125 + TOK],
                convw_sb[:, ac * KW:ac * KW + 1], None, ALU.mult),
                waits=[(sV, ev_ip[k])], inc=sV)
            for kk in range(1, KW):
                b.op("vector", lambda e, k=k, ac=ac, kk=kk:
                     e.scalar_tensor_tensor(
                         cacc[kk % 2][:],
                         evb[k % 2][:, 125 + kk:125 + kk + TOK],
                         convw_sb[:, ac * KW + kk:ac * KW + kk + 1],
                         cacc[(kk + 1) % 2][:], ALU.mult, ALU.add),
                     inc=(sV if kk == KW - 1 else None))
            CNV = sV.n
            wsil = [(sV, CNV)]
            if k >= 2 and (k - 2) in spill_ip:
                wsil.append((sSp[k % 2], spill_ip[k - 2]))
            b.op("scalar", lambda e, k=k, ac=ac: e.activation(
                evb[k % 2][:, 0:TOK], convacc2[:], AF.Silu,
                bias=convb_sb[:, ac:ac + 1]), waits=wsil, inc=sA)
            eplg[k] = (sA, sA.n)
            dst = sp_xc[(oc - 64) * 128:(oc - 63) * 128, :] if oc <= 127 \
                else sp_bc[(oc - 128) * 128:(oc - 127) * 128, :]
            b.dma("scalar", dst, evb[k % 2][:, 0:TOK], sSp[k % 2],
                  waits=[eplg[k]])
            spill_ip[k] = sSp[k % 2].n
        elif oc == OCH_DT:
            eplg[k] = (sV, ev_ip[k])
        else:
            wsil = [(sV, ev_ip[k])]
            if k >= 2 and (k - 2) in spill_ip:
                wsil.append((sSp[k % 2], spill_ip[k - 2]))
            b.op("scalar", lambda e, k=k: e.activation(
                evb[k % 2][:, HALO:TTK], evb[k % 2][:, HALO:TTK], AF.Silu),
                waits=wsil, inc=sA)
            eplg[k] = (sA, sA.n)
            b.dma("scalar", sp_gate[oc * 128:(oc + 1) * 128, :],
                  evb[k % 2][:, HALO:TTK], sSp[k % 2], waits=[eplg[k]])
            spill_ip[k] = sSp[k % 2].n

    dtproc = b.sb("dtproc", [128, TOK], F32)
    lndt = b.sb("lndt", [128, CHK], F32)
    cst = b.sb("cst", [128, TOK], F32)
    dtev = evb[k_dt % 2]
    dtw = convacc

    b.op("vector", lambda e: e.tensor_scalar(
        dtproc[:], dtev[:, HALO:TTK], dtb_sb[:, 0:1], 0.0, ALU.add, ALU.max),
        waits=[(sV, ev_ip[k_dt]), (sG, sG.n)], inc=sV)
    b.op("scalar", lambda e: e.activation(
        dtw[:], dtev[:, HALO:TTK], AF.Abs, bias=dtb_sb[:, 0:1]),
        waits=[(sV, ev_ip[k_dt]), (sG, sG.n)], inc=sA)
    b.op("scalar", lambda e: e.activation(dtw[:], dtw[:], AF.Exp,
         scale=cconst[:, 4:5]), inc=sA)
    b.op("scalar", lambda e: e.activation(dtw[:], dtw[:], AF.Ln,
         bias=cconst[:, 1:2]), inc=sA)
    DTL = sA.n
    b.op("vector", lambda e: e.tensor_tensor(dtproc[:], dtproc[:], dtw[:],
                                             ALU.add), waits=[(sA, DTL)],
         inc=sV)
    b.op("vector", lambda e: e.tensor_scalar(
        dtproc[:], dtproc[:], 1e-3, 100.0, ALU.max, ALU.min), inc=sV)
    DTP = sV.n
    for c in range(NCC):
        wl = [(sV, DTP)]
        if c >= 1:
            wl.append((sLd, 16 * c))
        b.op("scalar", lambda e, c=c: e.activation(
            lndt[:], dtproc[:, c * CHK:(c + 1) * CHK], AF.Ln), waits=wl,
            inc=sA)
        LNc = sA.n
        b.dma("scalar", sp_ldtf[c:c + 1, :].rearrange("o (h t) -> (o h) t", h=128),
              lndt[:], sLd, waits=[(sA, LNc)])
    LND = sA.n
    b.op("vector", lambda e: e.tensor_scalar(
        dtw[:], dtproc[:], acol[:, 0:1], None, ALU.mult), waits=[(sA, LND)],
        inc=sV)
    DA = sV.n
    for c in range(NCC):
        b.op("vector", lambda e, c=c: e.tensor_tensor_scan(
            cst[:, c * CHK:(c + 1) * CHK], ones128[:],
            dtw[:, c * CHK:(c + 1) * CHK], 0.0, ALU.mult, ALU.add),
            waits=[(sV, DA)], inc=sV)
    CS = sV.n
    for c in range(NCC):
        b.dma("scalar", sp_csf[c:c + 1, :].rearrange("o (h t) -> (o h) t", h=128),
              cst[:, c * CHK:(c + 1) * CHK], sBn, waits=[(sV, CS)])
        b.dma("scalar", sp_dtf[c:c + 1, :].rearrange("o (h t) -> (o h) t", h=128),
              dtproc[:, c * CHK:(c + 1) * CHK], sBn, waits=[(sV, DTP)])
    FLATS = sBn.n
    if DEBUG_XNFM:
        dxn = nc.dram_tensor("dbg_xnfm", [DCH * 128, TTK], BF16,
                             kind="ExternalOutput").ap()
        for a in range(DCH):
            b.dma("sync", dxn[a * 128:(a + 1) * 128, :], xnfm[a][:], sBn,
                  waits=[(sMM, mm_ip[k_dt])])
        FLATS = sBn.n
    b.op("vector", lambda e: None,
         waits=[(sBn, FLATS), (sLd, sLd.n), (sSp[0], sSp[0].n),
                (sSp[1], sSp[1].n)])
    barrier()
    b.free("cst", "lndt", "dtproc", "convacc2", "convacc", "evb1", "evb0",
           "wbufA2", "wbufA1", "wbufA0")

    wbufF = [b.sb(f"wbufF{i}", [128, DCH * 128], BF16) for i in range(3)]
    wg9sb = b.sb("wg9sb", [128, DCH * 9], BF16)
    bwT_sb = b.sb("bwT_sb", [128, DM], BF16)
    h_sb = b.sb("h_sb", [128, TOK], BF16)
    hw_sb = b.sb("hw_sb", [128, TOK], BF16)
    glsb = b.sb("glsb", [9, 512], F32)
    gltm = b.sb("gltm", [128, 9], F32)
    gcol = b.sb("gcol", [128, 32], F32)
    wnsb = b.sb("wnsb", [128, 8], F32)
    wnsb2 = b.sb("wnsb2", [128, 8], F32)
    wTst = b.sb("wTst", [8, 512], F32)
    gstage = [b.sb(f"gstage{i}", [128, TOK], BF16) for i in range(2)]

    b.dma("gpsimd", wg9sb[:], ap["wg9T"].rearrange("(a p) j -> p a j", p=128),
          sFW, waits=PW())
    b.dma("gpsimd", bwT_sb[:], ap["bwT"], sFW)
    b.dma("gpsimd", wbufF[0][:], ap["awT"].rearrange("(a p) r -> p a r", p=128),
          sFW)
    FW = sFW.n

    for ti in range(4):
        for a in range(DCH):
            b.op("tensor", lambda e, a=a, ti=ti: e.matmul(
                pT[:, ti * 512:(ti + 1) * 512],
                wbufF[0][:, a * 128:(a + 1) * 128],
                xnfm[a][:, HALO + ti * 512:HALO + (ti + 1) * 512],
                start=(a == 0), stop=(a == DCH - 1)),
                waits=([(sFW, FW)] + PW()) if (a == 0 and ti == 0) else [],
                inc=(sMM if (a == DCH - 1 and ti == 3) else None))
    HMM = sMM.n
    b.op("vector", lambda e: e.tensor_copy(h_sb[:], pT[:, 0:TOK]),
         waits=[(sMM, HMM)], inc=sV)
    HEV = sV.n

    for ti in range(4):
        for a in range(DCH):
            b.op("tensor", lambda e, a=a, ti=ti: e.matmul(
                pT[0:9, ti * 512:(ti + 1) * 512],
                wg9sb[:, a * 9:(a + 1) * 9],
                xnfm[a][:, HALO + ti * 512:HALO + (ti + 1) * 512],
                start=(a == 0), stop=(a == DCH - 1)),
                waits=[(sFW, FW), (sV, HEV)] if (a == 0 and ti == 0) else [],
                inc=(sMM if (a == DCH - 1 and ti == 3) else None))
    GL = sMM.n
    hw_mark = {}
    for ti in range(4):
        wev = [(sMM, GL)]
        b.op("vector", lambda e, ti=ti: e.tensor_copy(
            glsb[:], pT[0:9, ti * 512:(ti + 1) * 512]), waits=wev, inc=sV)
        GEV = sV.n
        for tc in range(4):
            b.op("tensor", lambda e, tc=tc: e.transpose(
                pFt[:, 0:9], glsb[:, tc * 128:(tc + 1) * 128],
                identf[0:9, 0:9]), waits=[(sV, GEV), (sV, sV.n)], inc=sMM)
            b.opf("vector", lambda e: e.tensor_copy(gltm[:], pFt[:, 0:9]),
                  waits=[(sMM, sMM.n)], inc=sV)
            b.opf("vector", lambda e: e.reduce_max(
                gcol[:, 0:1], gltm[:, 0:8], axis=AX.X), inc=sV)
            b.opf("vector", lambda e: e.tensor_scalar(
                gcol[:, 4:5], gcol[:, 0:1], -1.0, None, ALU.mult), inc=sV)
            NM = sV.n
            b.op("scalar", lambda e: e.activation(
                gltm[:, 0:8], gltm[:, 0:8], AF.Exp, bias=gcol[:, 4:5]),
                waits=[(sV, NM)], inc=sA)
            EX = sA.n
            b.opf("vector", lambda e: e.reduce_sum(
                gcol[:, 8:9], gltm[:, 0:8], axis=AX.X), waits=[(sA, EX)],
                inc=sV)
            b.opf("vector", lambda e: e.reciprocal(gcol[:, 12:13],
                                                   gcol[:, 8:9]), inc=sV)
            b.opf("scalar", lambda e: e.activation(
                gcol[:, 16:17], gltm[:, 8:9], AF.Sigmoid,
                bias=bthr_col[:, 0:1]), inc=sA)
            TH = sA.n
            b.opf("vector", lambda e: e.tensor_scalar(
                gcol[:, 28:29], gcol[:, 16:17], 0.125, None, ALU.mult),
                waits=[(sA, TH)], inc=sV)
            b.opf("vector", lambda e: e.tensor_scalar(
                wnsb[:], gltm[:, 0:8], gcol[:, 12:13], gcol[:, 28:29],
                ALU.mult, ALU.subtract), inc=sV)
            b.opf("vector", lambda e: e.tensor_scalar(
                wnsb2[:], wnsb[:], 0.0, None, ALU.max), inc=sV)
            b.opf("vector", lambda e: e.reduce_sum(
                gcol[:, 20:21], wnsb2[:], axis=AX.X), inc=sV)
            b.opf("vector", lambda e: e.tensor_scalar(
                gcol[:, 24:25], gcol[:, 20:21], 0.0, None, ALU.is_equal),
                inc=sV)
            b.opf("vector", lambda e: e.tensor_tensor(
                gcol[:, 20:21], gcol[:, 20:21], gcol[:, 24:25], ALU.add),
                inc=sV)
            b.opf("vector", lambda e: e.reciprocal(gcol[:, 12:13],
                                                   gcol[:, 20:21]), inc=sV)
            b.opf("vector", lambda e: e.tensor_scalar(
                wnsb[:], wnsb2[:], gcol[:, 12:13], None, ALU.mult), inc=sV)
            WN = sV.n
            b.op("tensor", lambda e, tc=tc: e.transpose(
                pFt[0:8, :], wnsb[:], identf[:]), waits=[(sV, WN)], inc=sMM)
            b.op("vector", lambda e, tc=tc: e.tensor_copy(
                wTst[:, tc * 128:(tc + 1) * 128], pFt[0:8, 0:128]),
                waits=[(sMM, sMM.n)], inc=sV)
        WT = sV.n
        wwbc = [(sV, WT)]
        if ti > 0:
            wwbc.append((sV, hw_mark[ti - 1]))
        b.op("tensor", lambda e, ti=ti: e.matmul(
            pB[:, 0:512], erep[:], wTst[:], start=True, stop=True),
            waits=wwbc, inc=sMM)
        WB = sMM.n
        b.op("vector", lambda e, ti=ti: e.scalar_tensor_tensor(
            hw_sb[:, ti * 512:(ti + 1) * 512],
            h_sb[:, ti * 512:(ti + 1) * 512], 2.0, pB[:, 0:512],
            ALU.mult, ALU.mult), waits=[(sMM, WB)], inc=sV)
        hw_mark[ti] = sV.n
    HW = sV.n

    mm_f = {}
    ev_f = {}
    for j in range(DCH):
        w = [(sV, HW)]
        if j >= 3:
            w.append((sMM, mm_f[j - 3]))
        b.dma("gpsimd", wbufF[j % 3][:], ap["wbT"][j], sW[j % 3], waits=w)
        WL = sW[j % 3].n
        for ti in range(4):
            for a in range(DCH):
                wmm = []
                if a == 0 and ti == 0:
                    wmm = [(sW[j % 3], WL)]
                    if j >= 1:
                        wmm.append((sA, ev_f[j - 1]))
                b.op("tensor", lambda e, a=a, ti=ti, j=j: e.matmul(
                    pT[:, ti * 512:(ti + 1) * 512],
                    wbufF[j % 3][:, a * 128:(a + 1) * 128],
                    xnfm[a][:, HALO + ti * 512:HALO + (ti + 1) * 512],
                    start=(a == 0), stop=False), waits=wmm)
        for ti in range(4):
            b.op("tensor", lambda e, ti=ti, j=j: e.matmul(
                pT[:, ti * 512:(ti + 1) * 512],
                bwT_sb[:, j * 128:(j + 1) * 128],
                hw_sb[:, ti * 512:(ti + 1) * 512], start=False, stop=True),
                inc=(sMM if ti == 3 else None))
        mm_f[j] = sMM.n
        wg = [(sMM, mm_f[j])]
        if j >= 2:
            wg.append((sSp[j % 2], spill_ip[("f", j - 2)]))
        b.op("scalar", lambda e, j=j: e.activation(
            gstage[j % 2][:], pT[:, 0:TOK], AF.Silu, bias=bb_sb[:, j:j + 1]),
            waits=wg, inc=sA)
        ev_f[j] = sA.n
        b.dma("scalar", sp_g[j * 128:(j + 1) * 128, :], gstage[j % 2][:],
              sSp[j % 2], waits=[(sA, ev_f[j])])
        spill_ip[("f", j)] = sSp[j % 2].n

    b.op("vector", lambda e: None, waits=[(sSp[0], sSp[0].n),
                                          (sSp[1], sSp[1].n)])
    barrier()
    b.free("gstage1", "gstage0", "wTst", "wnsb2", "wnsb", "gcol", "gltm", "glsb",
           "hw_sb", "h_sb", "bwT_sb", "wg9sb", "wbufF2", "wbufF1", "wbufF0")
    for a in reversed(range(DCH)):
        b.free(f"xnfm{a}")

    csT_all = b.sb("csT_all", [128, TOK], F32)
    ddtT_all = b.sb("ddtT_all", [128, TOK], F32)
    cdcol_all = b.sb("cdcol_all", [128, TOK], F32)
    cdtot = b.sb("cdtot", [128, 128], F32)
    carry = b.sb("carry", [128, INTER], F32)
    btm = b.sb("btm", [128, NCC * NG * 128], BF16)
    xtm = [b.sb(f"xtm{i}", [128, INTER], BF16) for i in range(2)]
    xdd = [b.sb(f"xdd{i}", [128, 512], BF16) for i in range(2)]
    sctmp = [b.sb(f"sctmp{i}", [128, 512], F32) for i in range(2)]
    csw = b.sb("csw", [128, 128], F32)
    dtwn = b.sb("dtwn", [128, 128], F32)
    cslr = b.sb("cslr", [1, 128], F32)
    cdrow = b.sb("cdrow", [1, 128], F32)
    dcyT = b.sb("dcyT", [128, 128], F32)

    b.op("vector", lambda e: e.memset(carry[:], 0.0), waits=PW(), inc=sV)
    b.op("vector", lambda e: e.memset(cdtot[:], 1.0), inc=sV)
    INITA = sV.n

    for c in range(NCC):
        b.dma("sync", btm[:, c * NG * 128:(c + 1) * NG * 128],
              sp_bc[0:NG * 128, c * CHK:(c + 1) * CHK], sTrB,
              waits=PW() if c == 0 else (), transpose=True)
    BTM = sTrB.n

    mkA = {}
    for c in range(NCC):
        wrb = PW()
        if c >= 1:
            wrb.append((sV, mkA[("ddt", c - 1)]))
        b.dma("sync", csw[:], sp_csf[c:c + 1, :]
              .rearrange("o (h t) -> (o h) t", h=128), sRb, waits=wrb)
        b.dma("sync", dtwn[:], sp_dtf[c:c + 1, :]
              .rearrange("o (h t) -> (o h) t", h=128), sRb)
        RB = sRb.n
        wtr = [(sRb, RB)]
        if c >= 1:
            wtr.append((sV, mkA[("cslr", c - 1)]))
        b.op("tensor", lambda e: e.transpose(pFt[:], csw[:], identf[:]),
             waits=wtr, inc=sMM)
        b.op("vector", lambda e, c=c: e.tensor_copy(
            csT_all[:, c * CHK:(c + 1) * CHK], pFt[:]),
            waits=[(sMM, sMM.n)], inc=sV)
        CST = sV.n
        b.op("tensor", lambda e: e.transpose(pFt[:], dtwn[:], identf[:]),
             waits=[(sV, CST)], inc=sMM)
        b.op("vector", lambda e: e.tensor_copy(dcyT[:], pFt[:]),
             waits=[(sMM, sMM.n)], inc=sV)
        mkA[("dtT", c)] = sV.n
        wcsl = [(sRb, RB), (sV, sV.n)]
        b.op("tensor", lambda e: e.transpose(
            pFt[0:1, 0:128], csw[:, 127:128], identf[:]), waits=wcsl,
            inc=sMM)
        CRT = sMM.n
        b.opf("vector", lambda e: e.tensor_copy(cslr[:], pFt[0:1, 0:128]),
              waits=[(sMM, CRT)], inc=sV)
        CR = sV.n
        mkA[("cslr", c)] = CR
        b.opf("scalar", lambda e: e.activation(cdrow[:], cslr[:], AF.Exp),
              waits=[(sV, CR)], inc=sA)
        CD = sA.n
        wbc0 = [(sA, CD)]
        if c >= 1:
            wbc0.append((sV, mkA[("ddt", c - 1)]))
        b.op("tensor", lambda e: e.matmul(pB[:, 0:128], ones1[:], cdrow[:],
                                          start=True, stop=False),
             waits=wbc0)
        b.op("tensor", lambda e: e.matmul(pB[:, 128:256], ones1[:], cslr[:],
                                          start=False, stop=True), inc=sMM)
        BC = sMM.n
        mkA[("bc", c)] = BC
        b.op("vector", lambda e, c=c: e.tensor_copy(
            cdcol_all[:, c * CHK:(c + 1) * CHK], pB[:, 0:128]),
            waits=[(sMM, BC)], inc=sV)
        b.op("vector", lambda e, c=c: e.tensor_tensor(
            ddtT_all[:, c * CHK:(c + 1) * CHK], pB[:, 128:256],
            csT_all[:, c * CHK:(c + 1) * CHK], ALU.subtract), inc=sV)
        DS = sV.n
        b.op("scalar", lambda e, c=c: e.activation(
            ddtT_all[:, c * CHK:(c + 1) * CHK],
            ddtT_all[:, c * CHK:(c + 1) * CHK], AF.Exp), waits=[(sV, DS)],
            inc=sA)
        DE = sA.n
        b.op("vector", lambda e, c=c: e.tensor_tensor(
            ddtT_all[:, c * CHK:(c + 1) * CHK],
            ddtT_all[:, c * CHK:(c + 1) * CHK], dcyT[:], ALU.mult),
            waits=[(sA, DE)], inc=sV)
        mkA[("ddt", c)] = sV.n
        DDT = sV.n
        b.op("vector", lambda e, c=c: e.tensor_tensor(
            cdtot[:], cdtot[:], cdcol_all[:, c * CHK:(c + 1) * CHK],
            ALU.mult), inc=sV)
        wx = PW()
        if c >= 2:
            wx.append((sMM, mkA[("stmm", c - 2, 15)]))
        b.dma("sync", xtm[c % 2][:], sp_xc[:, c * CHK:(c + 1) * CHK],
              sTrX[c % 2], waits=wx, transpose=True)
        XT = sTrX[c % 2].n

        for hb in range(16):
            h0 = hb * 8
            g = h0 // 16
            bank = pT[:, (hb % 2) * 512:(hb % 2) * 512 + 512]
            wxd = [(sTrX[c % 2], XT), (sV, DDT)]
            prev_stmm = mkA.get(("stmm", c, hb - 2),
                                mkA.get(("stmm", c - 1, 14 + hb)))
            if prev_stmm is not None:
                wxd.append((sMM, prev_stmm))
            b.op("vector", lambda e, c=c, h0=h0, hb=hb: e.tensor_tensor(
                r3(xdd[hb % 2][:], 8),
                r3(xtm[c % 2][:, h0 * HD:(h0 + 8) * HD], 8),
                bcast8(ddtT_all[:, c * CHK + h0:c * CHK + h0 + 8], 8, HD),
                ALU.mult), waits=wxd, inc=sV)
            XD = sV.n
            wst = [(sV, XD), (sTrB, BTM)]
            prev_scan = mkA.get(("scan", c, hb - 2),
                                mkA.get(("scan", c - 1, 14 + hb)))
            if prev_scan is not None:
                wst.append((sV, prev_scan))
            b.op("tensor", lambda e, c=c, g=g, hb=hb, bank=bank: e.matmul(
                bank, btm[:, c * NG * 128 + g * 128:c * NG * 128 + (g + 1) * 128],
                xdd[hb % 2][:], start=True, stop=True), waits=wst, inc=sMM)
            mkA[("stmm", c, hb)] = sMM.n
            wsc = [(sV, INITA)]
            if c >= 1:
                wsc = [(sV, mkA[("scan", c - 1, hb)])]
            prev_any = mkA.get(("scan", c, hb - 2),
                               mkA.get(("scan", c - 1, 14 + hb)))
            if prev_any is not None:
                wsc.append((sV, prev_any))
            b.op("vector", lambda e, c=c, h0=h0, hb=hb: e.tensor_tensor(
                r3(sctmp[hb % 2][:], 8),
                r3(carry[:, h0 * HD:(h0 + 8) * HD], 8),
                bcast8(cdcol_all[:, c * CHK + h0:c * CHK + h0 + 8], 8, HD),
                ALU.mult), waits=wsc, inc=sV)
            SC = sV.n
            b.op("vector", lambda e, h0=h0, hb=hb, bank=bank: e.tensor_tensor(
                carry[:, h0 * HD:(h0 + 8) * HD], sctmp[hb % 2][:], bank,
                ALU.add), waits=[(sV, SC), (sMM, mkA[("stmm", c, hb)])],
                inc=sV)
            mkA[("scan", c, hb)] = sV.n

    SCANF = sV.n
    b.dma("scalar", cc_in[:, 0:INTER], carry[:], sPk, waits=[(sV, SCANF)])
    b.dma("scalar", cc_in[:, INTER:INTER + 128], cdtot[:], sPk)
    PK = sPk.n
    b.op("gpsimd", lambda e: e.collective_compute(
        "AllGather", ALU.bypass, replica_groups=[[0, 1, 2, 3], [4, 5, 6, 7]],
        ins=[cc_in.opt()], outs=[cc_out.opt()]), waits=[(sPk, PK)], inc=sCC)
    CCD = sCC.n

    cd12 = b.sb("cd12", [128, 256], F32)
    ctmp = b.sb("ctmp", [128, TOK], F32)
    Fst = b.sb("Fst", [128, TOK], F32)
    Ast = b.sb("Ast", [128, TOK], F32)
    b.dma("sync", cd12[:, 0:128], cc_out[128:256, INTER:INTER + 128], sRb2,
          waits=[(sCC, CCD)])
    b.dma("sync", cd12[:, 128:256], cc_out[256:384, INTER:INTER + 128], sRb2)
    CD12 = sRb2.n
    for s4 in range(4):
        col = s4 * TOK
        h0 = s4 * 32
        b.dma("sync", Ast[:], cc_out[0:128, col:col + TOK], sRb2,
              waits=[(sV, sV.n), (sCC, CCD)])
        AL = sRb2.n
        b.op("vector", lambda e, col=col: e.tensor_scalar(
            carry[:, col:col + TOK], Ast[:], qsel_sb[:, 0:1], None,
            ALU.mult), waits=[(sRb2, AL)], inc=sV)
        for j in (1, 2):
            b.dma("sync", Fst[:], cc_out[j * 128:(j + 1) * 128, col:col + TOK],
                  sRb2, waits=[(sV, sV.n)])
            FL = sRb2.n
            b.op("vector", lambda e, j=j, h0=h0: e.tensor_tensor(
                r3(Ast[:], 32), r3(Ast[:], 32),
                bcast8(cd12[:, (j - 1) * 128 + h0:(j - 1) * 128 + h0 + 32],
                       32, HD), ALU.mult),
                waits=[(sRb2, FL), (sRb2, CD12)], inc=sV)
            b.op("vector", lambda e: e.tensor_tensor(
                Ast[:], Ast[:], Fst[:], ALU.add), inc=sV)
            b.op("vector", lambda e, j=j: e.tensor_scalar(
                ctmp[:], Ast[:], qsel_sb[:, j:j + 1], None, ALU.mult),
                inc=sV)
            b.op("vector", lambda e, col=col: e.tensor_tensor(
                carry[:, col:col + TOK], carry[:, col:col + TOK], ctmp[:],
                ALU.add), inc=sV)
    CINIT = sV.n
    barrier()
    b.free("Ast", "Fst", "ctmp", "cd12", "dcyT", "cdrow", "cslr", "dtwn", "csw")

    cslndtT = b.sb("cslndtT", [128, 128], F32)
    expw = b.sb("expw", [128, 128], F32)
    S_g = b.sb("S_g", [128, 128], F32)
    bcfm = b.sb("bcfm", [128, 2 * NG * 128], BF16)
    flat4 = [b.sb(f"flat4_{i}", [1, 512], F32) for i in range(2)]
    segb = [b.sb(f"segb{i}", [128, 512], F32) for i in range(2)]
    m4g = [b.sb(f"m4g{i}", [128, 2048], BF16) for i in range(2)]
    dx8 = [b.sb(f"dx8_{i}", [128, 512], BF16) for i in range(2)]
    cv8 = [b.sb(f"cv8_{i}", [128, 512], BF16) for i in range(2)]
    gtm = b.sb("gtm", [128, INTER], BF16)
    Yt = b.sb("Yt", [128, INTER], F32)
    rmsc = b.sb("rmsc", [128, 64], F32)

    mk = {}
    ydg_hist = {0: [], 1: []}
    seg_hist = []
    csbc_hist = []
    fl_hist = {0: [], 1: []}
    batch_hist = []

    for c in range(NCC):
        wrb = PW()
        if c >= 1:
            wrb.append((sMM, mk[("ltr", c - 1)]))
            wrb.append((sV, seg_hist[-1]))
        b.dma("sync", segb[0][:, 0:128], sp_ldtf[c:c + 1, :]
              .rearrange("o (h t) -> (o h) t", h=128), sRb, waits=wrb)
        LR = sRb.n
        wltr = [(sRb, LR)]
        if c >= 1:
            wltr.append((sV, mk[("pc", c - 1)]))
        b.op("tensor", lambda e: e.transpose(pFt[:], segb[0][:, 0:128],
                                             identf[:]), waits=wltr, inc=sMM)
        mk[("ltr", c)] = sMM.n
        b.op("vector", lambda e, c=c: e.scalar_tensor_tensor(
            cslndtT[:], pFt[:], -1.0, csT_all[:, c * CHK:(c + 1) * CHK],
            ALU.mult, ALU.add), waits=[(sMM, mk[("ltr", c)])], inc=sV)
        mk[("pc", c)] = sV.n
        CSL = sV.n
        wex = [(sV, CSL)]
        if c >= 1:
            wex.append((sV, mk[("mrgall", c - 1)]))
        b.op("scalar", lambda e, c=c: e.activation(
            expw[:], csT_all[:, c * CHK:(c + 1) * CHK], AF.Exp),
            waits=wex, inc=sA)
        EXW = sA.n
        wx = PW()
        if c >= 2:
            wx.append((sV, mk[("ygdone", c - 2)]))
        b.dma("sync", xtm[c % 2][:], sp_xc[:, c * CHK:(c + 1) * CHK],
              sTrX[c % 2], waits=wx, transpose=True)
        XT = sTrX[c % 2].n
        wg = PW()
        if c >= 1:
            wg.append((sV, mk[("ygdone", c - 1)]))
        b.dma("sync", gtm[:], sp_gate[:, c * CHK:(c + 1) * CHK], sTrG,
              waits=wg, transpose=True)
        GT = sTrG.n
        wbc = PW()
        if c >= 1:
            wbc.append((sMM, mk[("Slast", c - 1)]))
        b.dma("sync", bcfm[:], sp_bc[:, c * CHK:(c + 1) * CHK]
              .rearrange("(u n) t -> n u t", n=128), sRb2, waits=wbc)
        BCL = sRb2.n

        for g in range(NG):
            wS = [(sRb2, BCL)]
            if (c, g) != (0, 0):
                wS.append((sV, mk.get(("Sev", c, g - 1),
                                      mk.get(("Sev", c - 1, NG - 1)))))
            b.op("tensor", lambda e, g=g: e.matmul(
                pSS, bcfm[:, g * 128:(g + 1) * 128],
                bcfm[:, (NG + g) * 128:(NG + g + 1) * 128],
                start=True, stop=True), waits=wS, inc=sMM)
            SM = sMM.n
            if g == NG - 1:
                mk[("Slast", c)] = sMM.n
            wsev = [(sMM, SM)]
            if seg_hist:
                wsev.append((sV, seg_hist[-1]))
            b.op("vector", lambda e: e.tensor_copy(S_g[:], pSS),
                 waits=wsev, inc=sV)
            mk[("Sev", c, g)] = sV.n
            SEV = sV.n

            for q4 in range(4):
                h0 = g * 16 + q4 * 4
                pb = q4 % 2
                wfl = []
                if len(fl_hist[pb]) >= 1:
                    wfl.append((sMM, fl_hist[pb][-1]))
                b.dma("sync", flat4[pb][:],
                      sp_csf[c:c + 1, h0 * CHK:h0 * CHK + 512], sFl[pb],
                      waits=wfl)
                FLQ = sFl[pb].n
                wcb = [(sFl[pb], FLQ), (sV, MSET)]
                if len(csbc_hist) >= 2:
                    wcb.append((sV, csbc_hist[-2]))
                b.op("tensor", lambda e, pb=pb: e.matmul(
                    pT[:, pb * 512:pb * 512 + 512], ones1[:], flat4[pb][:],
                    start=True, stop=True), waits=wcb, inc=sMM)
                CBM = sMM.n
                fl_hist[pb].append(CBM)
                wsg = [(sMM, CBM), (sV, mk[("pc", c)])]
                if len(seg_hist) >= 2:
                    wsg.append((sV, seg_hist[-2]))
                b.op("vector", lambda e, pb=pb, h0=h0: e.tensor_tensor(
                    segb[pb][:].rearrange("p (a t) -> p a t", a=4),
                    pT[:, pb * 512:pb * 512 + 512]
                    .rearrange("p (a t) -> p a t", a=4),
                    cslndtT[:, h0:h0 + 4][:, :, None]
                    .broadcast_to([128, 4, 128]), ALU.subtract),
                    waits=wsg, inc=sV)
                SGT = sV.n
                csbc_hist.append(SGT)
                b.op("vector", lambda e, pb=pb: e.tensor_tensor(
                    segb[pb][:], segb[pb][:], maskM[:], ALU.min),
                    waits=[(sV, SGT)], inc=sV)
                MS = sV.n
                b.op("scalar", lambda e, pb=pb: e.activation(
                    segb[pb][:], segb[pb][:], AF.Exp), waits=[(sV, MS)],
                    inc=sA)
                EXP = sA.n
                wm4 = [(sA, EXP), (sV, SEV)]
                if ydg_hist[g % 2]:
                    wm4.append((sMM, ydg_hist[g % 2][-1]))
                b.op("vector", lambda e, pb=pb, g=g, q4=q4: e.tensor_tensor(
                    m4g[g % 2][:, q4 * 512:(q4 + 1) * 512]
                    .rearrange("p (a t) -> p a t", a=4),
                    segb[pb][:].rearrange("p (a t) -> p a t", a=4),
                    S_g[:, None, :].broadcast_to([128, 4, 128]), ALU.mult),
                    waits=wm4, inc=sV)
                mk[("m4", c, g, q4)] = sV.n
                seg_hist.append(sV.n)
            mk[("m4last", c)] = sV.n

            for hb in range(2):
                h0 = g * 16 + hb * 8
                prev = batch_hist[-1] if batch_hist else None
                wxd = [(sTrX[c % 2], XT), (sV, mkA[("ddt", c)])]
                if prev is not None:
                    wxd.append((sMM, prev["st"]))
                b.op("vector", lambda e, c=c, h0=h0, hb=hb: e.tensor_tensor(
                    r3(xdd[hb][:], 8),
                    r3(xtm[c % 2][:, h0 * HD:(h0 + 8) * HD], 8),
                    bcast8(ddtT_all[:, c * CHK + h0:c * CHK + h0 + 8], 8, HD),
                    ALU.mult), waits=wxd, inc=sV)
                wdx = [(sTrX[c % 2], XT)]
                if prev is not None:
                    wdx.append((sMM, prev["yd"]))
                b.op("vector", lambda e, c=c, h0=h0, hb=hb: e.tensor_tensor(
                    r3(dx8[hb][:], 8),
                    r3(xtm[c % 2][:, h0 * HD:(h0 + 8) * HD], 8),
                    bcast8(dcol_all[:, h0:h0 + 8], 8, HD), ALU.mult),
                    waits=wdx, inc=sV)
                wcv = [(sV, mk[("scan", c - 1, g, hb)] if c >= 1 else CINIT)]
                if prev is not None:
                    wcv.append((sMM, prev["yo"]))
                XDD5 = sV.n
                b.op("gpsimd", lambda e, h0=h0, hb=hb: e.tensor_copy(
                    cv8[hb][:], carry[:, h0 * HD:(h0 + 8) * HD]),
                    waits=wcv, inc=sG)
                GXD = sG.n
                wst = [(sV, XDD5), (sTrB, BTM)]
                if prev is not None:
                    wst.append((sV, prev["scan"]))
                b.op("tensor", lambda e, c=c, g=g, hb=hb: e.matmul(
                    pT[:, 2048:2560],
                    btm[:, c * NG * 128 + g * 128:c * NG * 128 + (g + 1) * 128],
                    xdd[hb][:], start=True, stop=True), waits=wst, inc=sMM)
                STM = sMM.n
                wyd = [(sV, XDD5)]
                if prev is not None:
                    wyd.append((sV, prev["mrg2"]))
                b.op("tensor", lambda e, hb=hb: e.matmul(
                    pT[:, 1024:1536], identb[:], dx8[hb][:], start=True,
                    stop=False), waits=wyd)
                for hh in range(8):
                    ig = hb * 8 + hh
                    b.op("tensor", lambda e, c=c, g=g, ig=ig, hh=hh: e.matmul(
                        pT[:, 1024 + hh * HD:1024 + (hh + 1) * HD],
                        m4g[g % 2][:, ig * 128:(ig + 1) * 128],
                        xtm[c % 2][:, (g * 16 + ig) * HD:
                                    (g * 16 + ig + 1) * HD],
                        start=False, stop=(hh == 7)),
                        waits=([(sV, mk[("m4", c, g, ig // 4)])]
                               if hh % 4 == 0 else []),
                        inc=(sMM if hh == 7 else None))
                YDM = sMM.n
                if hb == 1:
                    ydg_hist[g % 2].append(YDM)
                wyo = [(sG, GXD)]
                if prev is not None:
                    wyo.append((sV, prev["mrg1"]))
                for hh in range(8):
                    b.op("tensor", lambda e, g=g, hb=hb, hh=hh: e.matmul(
                        pT[:, 1536 + hh * HD:1536 + (hh + 1) * HD],
                        bcfm[:, (NG + g) * 128:(NG + g + 1) * 128],
                        cv8[hb][:, hh * HD:(hh + 1) * HD],
                        start=(hh == 0), stop=(hh == 7)),
                        waits=(wyo if hh == 0 else ()),
                        inc=(sMM if hh == 7 else None))
                YOM = sMM.n
                wsc2 = [(sG, GXD)]
                if prev is not None:
                    wsc2.append((sV, prev["scan"]))
                    wsc2.append((sV, prev["mrg2"]))
                b.op("vector", lambda e, c=c, h0=h0, hb=hb: e.tensor_tensor(
                    r3(sctmp[hb][:], 8),
                    r3(carry[:, h0 * HD:(h0 + 8) * HD], 8),
                    bcast8(cdcol_all[:, c * CHK + h0:c * CHK + h0 + 8], 8, HD),
                    ALU.mult), waits=wsc2, inc=sV)
                SC2 = sV.n
                b.op("vector", lambda e, h0=h0, hb=hb: e.tensor_tensor(
                    carry[:, h0 * HD:(h0 + 8) * HD], sctmp[hb][:],
                    pT[:, 2048:2560], ALU.add),
                    waits=[(sV, SC2), (sMM, STM), (sG, GXD)], inc=sV)
                mk[("scan", c, g, hb)] = sV.n
                SCN = sV.n
                b.op("vector", lambda e, h0=h0, hb=hb: e.tensor_tensor(
                    r3(sctmp[1 - hb][:], 8),
                    pT[:, 1536:2048].rearrange("p (a b) -> p a b", a=8),
                    expw[:, h0:h0 + 8][:, :, None].broadcast_to([128, 8, HD]),
                    ALU.mult), waits=[(sMM, YOM), (sA, EXW)], inc=sV)
                MR1 = sV.n
                wmr2 = [(sMM, YDM)]
                if c >= 1 and (g, hb) == (0, 0):
                    wmr2.append((sSpG, mk[("spill", c - 1)]))
                b.op("vector", lambda e, h0=h0, hb=hb: e.tensor_tensor(
                    Yt[:, h0 * HD:(h0 + 8) * HD], pT[:, 1024:1536],
                    sctmp[1 - hb][:], ALU.add), waits=wmr2, inc=sV)
                MR2 = sV.n
                batch_hist.append({"st": STM, "yd": YDM, "yo": YOM,
                                   "scan": SCN, "mrg1": MR1, "mrg2": MR2})
        mk[("mrgall", c)] = sV.n

        YD = mk[("mrgall", c)]
        b.op("vector", lambda e: e.tensor_tensor(Yt[:], Yt[:], gtm[:],
                                                 ALU.mult),
             waits=[(sV, YD), (sTrG, GT)], inc=sV)
        YG = sV.n
        for gr in range(8):
            b.op("scalar", lambda e, gr=gr: e.activation(
                gtm[:, 0:1024], Yt[:, gr * 1024:(gr + 1) * 1024], AF.Square,
                accum_out=rmsc[:, 4 * gr:4 * gr + 1]), waits=[(sV, YG)], inc=sA)
        SQA = sA.n
        b.opf("scalar", lambda e: e.activation(
            rmsc[:, 32:64].rearrange("p (a u) -> p a u", u=4)[:, :, 0:1],
            rmsc[:, 0:32].rearrange("p (a u) -> p a u", u=4)[:, :, 0:1],
            AF.Ln, bias=cconst[:, 0:1], scale=cconst[:, 2:3]),
            waits=[(sA, SQA)], inc=sA)
        b.opf("scalar", lambda e: e.activation(
            rmsc[:, 32:64].rearrange("p (a u) -> p a u", u=4)[:, :, 0:1],
            rmsc[:, 32:64].rearrange("p (a u) -> p a u", u=4)[:, :, 0:1],
            AF.Exp, scale=cconst[:, 3:4]), inc=sA)
        RSD = sA.n
        for gr in range(8):
            b.op("vector", lambda e, gr=gr: e.tensor_scalar(
                Yt[:, gr * 1024:(gr + 1) * 1024],
                Yt[:, gr * 1024:(gr + 1) * 1024], rmsc[:, 32 + 4 * gr:33 + 4 * gr],
                None, ALU.mult), waits=[(sA, RSD)], inc=sV)
        mk[("ygdone", c)] = sV.n
        YGN = sV.n
        wsp = [(sV, YGN)]
        if c >= 1:
            wsp.append((sSpG, mk[("spill", c - 1)]))
        b.dma("gpsimd", sp_ygn[c * CHK:(c + 1) * CHK, :], Yt[:], sSpG,
              waits=wsp)
        mk[("spill", c)] = sSpG.n

    b.op("vector", lambda e: None, waits=[(sSpG, sSpG.n)])
    barrier()
    b.free("rmsc", "Yt", "gtm", "cv8_1", "cv8_0", "dx8_1",
           "dx8_0", "m4g1", "m4g0", "segb1", "segb0", "flat4_1", "flat4_0",
           "bcfm", "S_g", "expw", "cslndtT")
    b.free("sctmp1", "sctmp0", "xdd1", "xdd0", "xtm1", "xtm0", "btm", "carry",
           "cdtot", "cdcol_all", "ddtT_all", "csT_all")

    wbufO = [b.sb(f"wbufO{i}", [128, DCH * 128], BF16) for i in range(3)]
    ytl = [b.sb(f"ytl{a}", [128, TOK], BF16) for a in range(DCH)]
    prtb = [b.sb(f"prtb{i}", [128, TOK], F32) for i in range(2)]
    grb = b.sb("grb", [128, TOK], BF16)
    xrb = b.sb("xrb", [128, TOK], F32)
    ssb = b.sb("ssb", [128, TOK], F32)
    osb = b.sb("osb", [128, TOK], F32)

    mm_op = {}
    ev_op = {}
    for p in range(2):
        for a in range(DCH):
            wl = PW()
            if p == 1:
                wl.append((sMM, mm_op[(0, DCH - 1)]))
                wl.append((sV, ev_op[(0, DCH - 1)]))
            b.dma("sync", ytl[a][:],
                  sp_ygn[:, p * DM + a * 128:p * DM + (a + 1) * 128], sYL,
                  waits=(wl if a == 0 else ()), transpose=True)
        YLD = sYL.n
        for a in range(DCH):
            b.op("vector", lambda e, a=a, p=p: e.tensor_scalar(
                ytl[a][:], ytl[a][:],
                normw_sb[:, p * DCH + a:p * DCH + a + 1], None, ALU.mult),
                waits=[(sYL, YLD)], inc=sV)
        NWS = sV.n
        for j in range(DCH):
            w = [(sV, NWS)]
            if j >= 3:
                w.append((sMM, mm_op[(p, j - 3)]))
            elif p == 1:
                w.append((sMM, mm_op[(0, DCH - 3 + j)]))
            b.dma("gpsimd", wbufO[j % 3][:], ap["wopT"][p, j], sW[j % 3],
                  waits=w)
            WL = sW[j % 3].n
            for ti in range(4):
                for a in range(DCH):
                    wmm = []
                    if a == 0 and ti == 0:
                        wmm = [(sW[j % 3], WL)]
                        pe = ev_op.get((p, j - 1),
                                       ev_op.get((0, DCH - 1)) if p == 1 else None)
                        if pe is not None:
                            wmm.append((sV, pe))
                    b.op("tensor", lambda e, a=a, ti=ti, j=j: e.matmul(
                        pT[:, ti * 512:(ti + 1) * 512],
                        wbufO[j % 3][:, a * 128:(a + 1) * 128],
                        ytl[a][:, ti * 512:(ti + 1) * 512],
                        start=(a == 0), stop=(a == DCH - 1)), waits=wmm,
                        inc=(sMM if (a == DCH - 1 and ti == 3) else None))
            mm_op[(p, j)] = sMM.n
            if p == 0:
                wev = [(sMM, mm_op[(p, j)])]
                if j >= 2:
                    wev.append((sSp[j % 2], mk[("psp", j - 2)]))
                b.op("vector", lambda e, j=j: e.tensor_copy(
                    prtb[j % 2][:], pT[:, 0:TOK]), waits=wev, inc=sV)
                ev_op[(p, j)] = sV.n
                b.dma("scalar", sp_part[j * 128:(j + 1) * 128, :],
                      prtb[j % 2][:], sSp[j % 2], waits=[(sV, ev_op[(p, j)])])
                mk[("psp", j)] = sSp[j % 2].n
            else:
                wrb = [(sV, ev_op.get((1, j - 1), NWS))]
                if j == 0:
                    wrb += [(sSp[0], sSp[0].n), (sSp[1], sSp[1].n)]
                b.dma("sync", prtb[j % 2][:],
                      sp_part[j * 128:(j + 1) * 128, :], sRb, waits=wrb)
                b.dma("sync", grb[:], sp_g[j * 128:(j + 1) * 128, :], sRb)
                b.dma("sync", xrb[:], ap["xfm"][j * 128:(j + 1) * 128, :],
                      sRb)
                PR = sRb.n
                wfin = [(sMM, mm_op[(p, j)]), (sRb, PR)]
                b.op("vector", lambda e, j=j: e.tensor_tensor(
                    ssb[:], pT[:, 0:TOK], prtb[j % 2][:], ALU.add),
                    waits=wfin, inc=sV)
                b.op("vector", lambda e: e.tensor_tensor(
                    ssb[:], ssb[:], grb[:], ALU.mult), inc=sV)
                wos = []
                if j >= 1:
                    wos.append((sOut, mk[("out", j - 1)]))
                b.op("vector", lambda e, j=j: e.tensor_tensor(
                    osb[:], ssb[:], xrb[:], ALU.add), waits=wos, inc=sV)
                ev_op[(p, j)] = sV.n
                b.dma("scalar", ap_out[j * 128:(j + 1) * 128, :],
                      osb[:], sOut, waits=[(sV, ev_op[(p, j)])])
                mk[("out", j)] = sOut.n

    for o, src in dbg_outs:
        b.dma("sync", o, src, sOut,
              waits=[(sOut, sOut.n), (sSpG, sSpG.n), (sSp[0], sSp[0].n),
                     (sSp[1], sSp[1].n), (sBn, sBn.n)])
    b.op("scalar", lambda e: None, waits=[(sOut, sOut.n)])

    b.finalize()
    return nc


def prep_inputs(inputs):
    g = {k: np.asarray(v) for k, v in inputs.items()}
    x = np.ascontiguousarray(g["L0_attn_hidden_states"], dtype=np.float32)
    B, S, D = x.shape
    xf = x.reshape(B * S, D)

    _wip = g["L0_attn_in_proj_w"].T.astype(np.float32)
    wipT = np.ascontiguousarray(
        _wip.reshape(DCH, 128, PROJ // 128, 128).transpose(2, 1, 0, 3))
    _wop = g["L0_attn_out_proj_w"].T.astype(np.float32)
    wopT = np.ascontiguousarray(
        _wop.reshape(2, DCH, 128, DM // 128, 128).transpose(0, 3, 2, 1, 4))
    _wb = g["L0_ffn_W_base"].T.astype(np.float32)
    wbT = np.ascontiguousarray(
        _wb.reshape(DCH, 128, DM // 128, 128).transpose(2, 1, 0, 3))
    convw = np.ascontiguousarray(
        g["L0_attn_conv_w"].reshape(CONVD, KW).astype(np.float32))
    convb = np.ascontiguousarray(g["L0_attn_conv_b"].astype(np.float32).reshape(CONVD // 128, 128).T)
    dtb = np.ascontiguousarray(g["L0_attn_dt_bias"].astype(np.float32).reshape(128, 1))
    dD = np.ascontiguousarray(g["L0_attn_D"].astype(np.float32))
    normw = np.ascontiguousarray(g["L0_attn_norm_w"].astype(np.float32).reshape(INTER // 128, 128).T)
    wg9T = np.ascontiguousarray(np.concatenate(
        [g["L0_ffn_W_gate"].T, g["L0_ffn_W_thr"][:, None]],
        axis=1).astype(np.float32))
    bthr = np.ascontiguousarray(g["L0_ffn_b_thr"].astype(np.float32))
    awT = np.ascontiguousarray(
        g["L0_ffn_A_w"].reshape(128, DM).T.astype(np.float32))
    bwT = np.ascontiguousarray(
        g["L0_ffn_B_w"].transpose(0, 2, 1).reshape(128, DM).astype(np.float32))
    bbase = np.ascontiguousarray(g["L0_ffn_b_base"].astype(np.float32).reshape(DM // 128, 128).T)

    in_maps = []
    for core in range(8):
        seq, q = core // 4, core % 4
        t0 = seq * S + q * TOK
        own = xf[t0:t0 + TOK]
        halo = np.zeros((HALO, D), np.float32) if q == 0 else \
            xf[t0 - HALO:t0]
        xc = np.ascontiguousarray(np.concatenate([halo, own], axis=0))
        xfm = np.ascontiguousarray(own.T)
        qsel = np.zeros((128, 3), np.float32)
        if q >= 1:
            qsel[:, q - 1] = 1.0
        in_maps.append({
            "x": xc, "xfm": xfm, "wipT": wipT, "wopT": wopT, "wbT": wbT,
            "convw": convw, "convb": convb, "dtb": dtb, "dD": dD,
            "normw": normw, "wg9T": wg9T, "bthr": bthr, "awT": awT,
            "bwT": bwT, "bbase": bbase, "qsel3": qsel,
        })
    return in_maps, (B, S, D)


def kernel(**inputs):
    in_maps, (B, S, D) = prep_inputs(inputs)
    nc = build()
    res = run_bass_kernel_spmd(nc, in_maps, core_ids=list(range(8)))
    outs = [np.asarray(r["out"]).T for r in res.results]
    full = np.concatenate(outs, axis=0).reshape(B, S, D)
    return full.astype(np.float32)
